# version 04
# baseline (speedup 1.0000x reference)
import math
from contextlib import ExitStack

import numpy as np
import ml_dtypes
import concourse.bass as bass
import concourse.mybir as mybir
from concourse.alu_op_type import AluOpType as OP
from concourse.bass_utils import run_bass_kernel_spmd

F32 = mybir.dt.float32
BF16 = mybir.dt.bfloat16
AF = mybir.ActivationFunctionType
AX = mybir.AxisListType

MASK_BIG = 1e9
LN_EPS = 1e-5

ENGINES = ("sync", "gpsimd", "vector", "scalar", "tensor")

SELF_SYNC_METHS = {"tensor_reduce", "reduce_max", "reduce_sum", "bn_stats",
                   "bn_aggr", "tensor_tensor_scan", "max_index", "reciprocal",
                   "stream_shuffle"}



def tile_lhsT(wT: np.ndarray) -> np.ndarray:
    K, M = wT.shape
    nk, nm = -(-K // 128), -(-M // 128)
    H = np.zeros((nk * 128, nm * 128), np.float32)
    H[:K, :M] = wT
    H = H.reshape(nk, 128, nm, 128).transpose(2, 1, 0, 3).reshape(nm, 128, nk * 128)
    return np.ascontiguousarray(H)


def rhs_weight(w: np.ndarray) -> np.ndarray:
    K, N = w.shape
    nk = -(-K // 128)
    H = np.zeros((nk * 128, N), np.float32)
    H[:K, :] = w
    return np.ascontiguousarray(H)


def fm_cols(vec: np.ndarray) -> np.ndarray:
    C = vec.shape[0]
    assert C % 128 == 0
    return np.ascontiguousarray(vec.reshape(C // 128, 128).T.astype(np.float32))


def causal_maskbig(Q=128) -> np.ndarray:
    m = np.zeros((Q, Q), np.float32)
    m[np.tril_indices(Q, -1)] = MASK_BIG
    return m



class Sem:
    __slots__ = ("h", "n", "name")

    def __init__(self, h, name):
        self.h, self.n, self.name = h, 0, name


class MK:

    def __init__(self, nc):
        self.nc = nc
        self.ctx = ExitStack()
        self._sems = {}
        self._bufs = {}
        self.streams = {e: [] for e in ENGINES}
        self.inputs = {}

    def sem(self, name) -> Sem:
        if name not in self._sems:
            h = self.ctx.enter_context(self.nc.semaphore(name))
            self._sems[name] = Sem(h, name)
        return self._sems[name]

    def sbuf(self, name, shape, dtype):
        if name not in self._bufs:
            self._bufs[name] = self.ctx.enter_context(
                self.nc.sbuf_tensor(name, list(shape), dtype))
        return self._bufs[name]

    def psum(self, name, shape, dtype=F32):
        if name not in self._bufs:
            self._bufs[name] = self.ctx.enter_context(
                self.nc.psum_tensor(name, list(shape), dtype))
        return self._bufs[name]

    def dram_in(self, name, arr):
        if name not in self._bufs:
            a = np.ascontiguousarray(arr)
            dt = {np.dtype(np.float32): F32,
                  np.dtype(ml_dtypes.bfloat16): BF16}[a.dtype]
            self._bufs[name] = self.nc.dram_tensor(name, list(a.shape), dt,
                                                   kind="ExternalInput")
            self.inputs[name] = a
        return self._bufs[name]

    def dram_out(self, name, shape, dtype):
        if name not in self._bufs:
            self._bufs[name] = self.nc.dram_tensor(name, list(shape), dtype,
                                                   kind="ExternalOutput")
        return self._bufs[name]

    def dram_tmp(self, name, shape, dtype):
        if name not in self._bufs:
            self._bufs[name] = self.nc.dram_tensor(name, list(shape), dtype)
        return self._bufs[name]

    def buf(self, name):
        return self._bufs[name]

    def op(self, eng, meth, *args, wait=None, wait2=None, inc=None, incn=1, **kw):
        selfsync = (meth in SELF_SYNC_METHS) or (kw.get("accum_out") is not None)
        if selfsync and inc is None:
            inc = self.sem("selfs_" + eng)
        waits = []
        for w in (wait, wait2):
            if w is not None:
                waits.append((w.h, w.n))
        it = None
        if inc is not None:
            inc.n += incn
            it = (inc.h, incn)

        def f(e, meth=meth, args=args, kw=kw, waits=waits, it=it):
            for (h, n) in waits:
                e.wait_ge(h, n)
            inst = getattr(e, meth)(*args, **kw)
            if it is not None:
                inst.then_inc(it[0], it[1])
        self.streams[eng].append(f)
        if selfsync:
            self.waitop(eng, inc)

    def raw(self, eng, fn):
        self.streams[eng].append(fn)

    def waitop(self, eng, sem, n=None):
        tgt = sem.n if n is None else n

        def f(e, h=sem.h, tgt=tgt):
            e.wait_ge(h, tgt)
        self.streams[eng].append(f)

    def gap(self, eng, n=3):
        g = self.sbuf("gap_scr", (128, 512), F32)
        for _ in range(n):
            if eng == "vector":
                self.op("vector", "tensor_copy", g[:, :], g[:, :])
            elif eng == "scalar":
                self.op("scalar", "copy", g[:, :], g[:, :])
            elif eng == "gpsimd":
                self.op("gpsimd", "memset", g[:, :], 0.0)

    def gwait(self, eng, sem, n=3):
        self.waitop(eng, sem)
        self.gap(eng, n)

    def poison_all(self, value=-1e30):
        sg = self.sem("sg")
        for name, h in list(self._bufs.items()):
            try:
                shape = h.shape
                dt = h.dtype
            except Exception:
                continue
            if str(h.__class__.__name__) not in ("SBTensorHandle",):
                continue
            if dt not in (F32, BF16):
                continue
            self.op("gpsimd", "memset", h[tuple(slice(0, n) for n in shape)],
                    value, inc=sg)
        for e in ("vector", "scalar", "tensor", "sync"):
            self.waitop(e, sg)

    def ln_stats(self, src_ap, nm_dst, rs_dst, width, sv, ss, wait=None,
                 gaps=False):
        b = self.buf
        self.sbuf("bn_sc", (128, 16), F32)
        nchunk = width // 512
        for i in range(nchunk):
            self.op("vector", "bn_stats", b("bn_sc")[:, i * 6:(i + 1) * 6],
                    src_ap[:, i * 512:(i + 1) * 512],
                    wait=(wait if i == 0 else None))
        self.op("vector", "bn_aggr", b("bn_sc")[:, 12:14],
                b("bn_sc")[:, 0:6 * nchunk])
        self.op("vector", "tensor_scalar", nm_dst, b("bn_sc")[:, 12:13],
                -1.0, None, op0=OP.mult, inc=sv)
        if gaps:
            self.gwait("scalar", sv)
            self.op("scalar", "activation", b("bn_sc")[:, 14:15],
                    b("bn_sc")[:, 13:14], AF.Ln, bias=b("c_eps")[:, :])
        else:
            self.op("scalar", "activation", b("bn_sc")[:, 14:15],
                    b("bn_sc")[:, 13:14], AF.Ln, wait=sv, bias=b("c_eps")[:, :])
        self.op("scalar", "activation", rs_dst, b("bn_sc")[:, 14:15],
                AF.Exp, scale=-0.5, inc=ss)

    def dma(self, q, out, in_, sem, **kw):
        self.op(q, "dma_start", wait=kw.pop("wait", None), wait2=kw.pop("wait2", None),
                inc=sem, incn=16, out=out, in_=in_, **kw)

    def finalize_and_emit(self, block):
        for e in ENGINES:
            ops = self.streams[e]
            if not ops:
                continue

            def body(eng, ops=ops):
                for f in ops:
                    f(eng)
            getattr(block, e)(body)



class L0Attn:

    def __init__(self, mk: MK, tag="", dbg=False):
        self.mk = mk
        self.tag = tag
        self.dbg = dbg

    def dump(self, name, ap, shape, dtype):
        mk = self.mk
        h = mk.dram_out("dbg_" + name, shape, dtype)
        for s in ("sv", "ss", "sp", "sg"):
            mk.waitop("sync", mk.sem(s))
        mk.dma("sync", h[tuple(slice(0, n) for n in shape)], ap, mk.sem("so"))

    @staticmethod
    def prep_inputs(inp: dict) -> dict:
        out = {}
        out["l0_w_inproj"] = tile_lhsT(inp["L0_attn_in_proj_w"].T)
        cw = inp["L0_attn_conv_w"].reshape(2176, 4)
        cwf = np.zeros((128, 17 * 4), np.float32)
        for c in range(17):
            cwf[:, c * 4:(c + 1) * 4] = cw[c * 128:(c + 1) * 128, :]
        out["l0_convw"] = cwf
        out["l0_convb"] = fm_cols(inp["L0_attn_conv_b"])
        out["l0_dtbias"] = inp["L0_attn_dt_bias"].reshape(32, 1).astype(np.float32)
        out["l0_alog"] = inp["L0_attn_A_log"].reshape(32, 1).astype(np.float32)
        out["l0_d128"] = fm_cols(np.repeat(inp["L0_attn_D"], 64))
        out["l0_normw"] = fm_cols(inp["L0_attn_norm_w"])
        out["l0_w_outproj"] = tile_lhsT(inp["L0_attn_out_proj_w"].T)
        out["l0_w_toh"] = rhs_weight(inp["L0_attn_to_h_w"][:1024].T)
        out["l0f_ln1w"] = fm_cols(inp["L0_ffn_ln1_w"])
        out["l0f_ln1b"] = fm_cols(inp["L0_ffn_ln1_b"])
        out["l0f_w_qkv"] = rhs_weight(inp["L0_ffn_in_proj_w"].T)
        out["l0f_qkvb"] = inp["L0_ffn_in_proj_b"].reshape(1, 3072).astype(np.float32)
        out["cmask"] = causal_maskbig()
        out["ident_bf"] = np.eye(128, dtype=ml_dtypes.bfloat16)
        out["ident_f"] = np.eye(128, dtype=np.float32)
        return out

    def alloc(self):
        mk = self.mk
        sb = mk.sbuf
        sb("ident_bf_s", (128, 128), BF16)
        sb("ident_f_s", (128, 128), F32)
        sb("cmask_s", (128, 128), F32)
        sb("ones_f", (128, 128), F32)
        sb("ones_bf", (128, 128), BF16)
        sb("convw_s", (128, 68), F32)
        sb("convb_s", (128, 17), F32)
        sb("dtbias_s", (32, 1), F32)
        sb("acol_s", (32, 1), F32)
        sb("abc_s", (128, 32), F32)
        sb("d128_s", (128, 16), F32)
        sb("normw_s", (128, 16), F32)
        sb("ln1w_s", (128, 8), F32)
        sb("ln1b_s", (128, 8), F32)
        sb("qkvb_s", (1, 3072), BF16)
        sb("hrun", (64, 32 * 64), F32)
        sb("xbc", (128, 17 * 515), BF16)
        sb("zf", (128, 16 * 512), BF16)
        sb("xf", (128, 16 * 512), BF16)
        sb("bcu", (128, 512), BF16)
        sb("c0", (64, 512), BF16)
        sb("xtm", (128, 4 * 2048), BF16)
        sb("btm", (128, 4 * 64), BF16)
        sb("dtq", (32, 512), F32)
        sb("dcum", (32, 512), F32)
        sb("w2r", (64, 512), F32)
        sb("dtT", (128, 4 * 32), F32)
        sb("DtT", (128, 4 * 32), F32)
        sb("stm", (128, 4 * 32), F32)
        sb("dbc", (128, 4 * 32), F32)
        sb("yasm", (128, 16 * 128), F32)
        sb("ygf", (128, 16 * 512), BF16)
        sb("x1q", (128, 4 * 1024), F32)
        sb("tmq", (128, 4 * 1024), BF16)
        sb("fmq", (128, 8 * 512), BF16)
        sb("hf", (128, 8 * 512), BF16)
        sb("wbuf", (128, 8192), BF16)
        sb("wbuf2", (128, 8 * 1024), BF16)
        sb("sc_big", (128, 1024), F32)
        sb("sc_a", (128, 128), F32)
        sb("sc_l", (128, 128), F32)
        sb("sc_g", (128, 128), F32)
        sb("sc_m", (128, 128), BF16)
        sb("sc_cw", (64, 128), BF16)
        sb("sc_cb", (128, 128), F32)
        sb("sc_cb2", (128, 64), F32)
        sb("w2t", (128, 128), F32)
        sb("sc_sh", (32, 512), F32)
        sb("sc_bsc", (128, 64), BF16)
        sb("sc_sz", (128, 128), F32)
        sb("sc_small", (64, 64), F32)
        sb("ssrow", (1, 512), F32)
        sb("ssacc", (1, 128), F32)
        sb("stat_nm", (128, 16), F32)
        sb("stat_rs", (128, 16), F32)
        sb("stat_t1", (128, 1), F32)
        sb("stat_t2", (128, 1), F32)
        mk.psum("ps_mm", (128, 512), F32)
        mk.psum("ps_mm2", (128, 512), F32)
        mk.psum("ps_tr", (128, 1024), BF16)
        mk.psum("ps_trf", (128, 512), F32)
        mk.psum("ps_row", (128, 512), F32)
        mk.psum("ps_y", (128, 512), F32)
        mk.psum("ps_g", (128, 512), F32)
        mk.psum("ps_h", (128, 512), F32)

    def load_consts(self):
        mk = self.mk
        b = mk.buf
        ld = mk.sem("c_ld")
        sv = mk.sem("sv")
        ss = mk.sem("ss")
        sp = mk.sem("sp")
        for nm, dst in [("ident_bf", "ident_bf_s"), ("ident_f", "ident_f_s"),
                        ("cmask", "cmask_s"), ("l0_convw", "convw_s"),
                        ("l0_convb", "convb_s"), ("l0_dtbias", "dtbias_s"),
                        ("l0_d128", "d128_s"), ("l0_normw", "normw_s"),
                        ("l0f_ln1w", "ln1w_s"), ("l0f_ln1b", "ln1b_s")]:
            mk.dma("sync", b(dst)[:, :], b(nm).ap(), ld)
        mk.dma("gpsimd", b("qkvb_s")[:, :], b("l0f_qkvb").ap(), mk.sem("lw"))
        mk.dma("sync", b("acol_s")[:, :], b("l0_alog").ap(), ld)
        mk.sbuf("c_eps", (128, 1), F32)
        mk.sbuf("c_one", (128, 1), F32)
        mk.op("gpsimd", "memset", b("c_eps")[:, :], LN_EPS)
        mk.op("gpsimd", "memset", b("c_one")[:, :], 1.0)
        mk.op("gpsimd", "memset", b("ones_f")[:, :], 1.0)
        mk.op("gpsimd", "memset", b("ones_bf")[:, :], 1.0, inc=mk.sem("sg"))
        mk.waitop("scalar", ld)
        mk.op("scalar", "activation", b("acol_s")[:, :], b("acol_s")[:, :], AF.Exp,
              inc=ss)
        mk.op("vector", "tensor_scalar", b("acol_s")[:, :], b("acol_s")[:, :],
              -1.0, None, wait=ss, op0=OP.mult, inc=sv)
        mk.op("tensor", "transpose", mk.buf("ps_trf")[0:1, 0:32],
              b("acol_s")[:, :], b("ident_f_s")[0:32, 0:32], wait=sv, inc=sp)
        mk.op("vector", "tensor_copy", b("sc_small")[0:1, 0:32],
              mk.buf("ps_trf")[0:1, 0:32], wait=sp, inc=sv)
        mk.op("tensor", "matmul", mk.buf("ps_row")[:, 0:32],
              b("ones_f")[0:1, :], b("sc_small")[0:1, 0:32], wait=sv,
              start=True, stop=True, inc=sp)
        mk.op("vector", "tensor_copy", b("abc_s")[:, :],
              mk.buf("ps_row")[:, 0:32], wait=sp, inc=sv)
        mk.op("gpsimd", "memset", b("hrun")[:, :], 0.0, inc=mk.sem("sg"))
        mk.op("gpsimd", "memset", b("hrun_bf")[:, :], 0.0, inc=mk.sem("sg"))
        xbc = b("xbc")
        for ch in range(17):
            mk.op("gpsimd", "memset", xbc[:, ch * 515: ch * 515 + 3], 0.0,
                  inc=mk.sem("sg"))
        for e in ("vector", "scalar", "tensor"):
            mk.waitop(e, mk.sem("c_ld"))
            mk.waitop(e, mk.sem("lw"))
            mk.waitop(e, mk.sem("sg"))

    def quarter(self, q, x_hbm, x1_hbm, q_hbm, kv_hbm):
        mk = self.mk
        b = mk.buf
        ld = mk.sem("ld")
        lw = mk.sem("lw")
        so = mk.sem("so")
        sv = mk.sem("sv")
        ss = mk.sem("ss")
        sp = mk.sem("sp")
        sg = mk.sem("sg")
        TQ = 512
        r0 = q * TQ

        ps_mm, ps_mm2 = b("ps_mm"), b("ps_mm2")
        ps_tr, ps_trf = b("ps_tr"), b("ps_trf")
        ps_row, ps_y, ps_g, ps_h = b("ps_row"), b("ps_y"), b("ps_g"), b("ps_h")

        mk.waitop("vector", sp)
        for tc in range(4):
            xs = b("sc_big")
            mk.dma("sync", xs[:, :],
                   x_hbm[r0 + tc * 128: r0 + (tc + 1) * 128, :], ld,
                   wait=sv, wait2=ss)
            mk.ln_stats(xs, b("stat_nm")[:, tc:tc + 1], b("stat_rs")[:, tc:tc + 1],
                        1024, sv, ss, wait=ld)
            mk.op("vector", "tensor_scalar", b("tmq")[:, tc * 1024:(tc + 1) * 1024],
                  xs[:, :], b("stat_nm")[:, tc:tc + 1], b("stat_rs")[:, tc:tc + 1],
                  wait=ss, op0=OP.add, op1=OP.mult, inc=sv)
        for tc in range(4):
            for k in range(8):
                mk.op("tensor", "transpose", ps_tr[:, 0:128],
                      b("tmq")[:, tc * 1024 + k * 128: tc * 1024 + (k + 1) * 128],
                      b("ident_bf_s")[:, :], wait=sv, wait2=ss, inc=sp)
                mk.op("scalar", "copy",
                      b("fmq")[:, k * 512 + tc * 128: k * 512 + (tc + 1) * 128],
                      ps_tr[:, 0:128], wait=sp, inc=ss)

        if self.dbg and q == 0:
            self.dump("fmq_lnx", b("fmq")[:, :], (128, 4096), BF16)
            self.dump("stat_nm", b("stat_nm")[:, :], (128, 16), F32)
            self.dump("stat_rs", b("stat_rs")[:, :], (128, 16), F32)
            self.dump("tmq_lnx", b("tmq")[:, :], (128, 4096), BF16)
        winp = b("l0_w_inproj")
        G = 8
        mk.waitop("scalar", sv)
        for g0 in range(0, 34, G):
            g1 = min(g0 + G, 34)
            mk.waitop("gpsimd", sp)
            for mc in range(g0, g1):
                mk.dma("gpsimd", b("wbuf")[:, (mc - g0) * 1024:(mc - g0 + 1) * 1024],
                       winp.ap()[mc, :, :], lw)
            mk.waitop("tensor", lw)
            for mc in range(g0, g1):
                ps = ps_mm if (mc % 2 == 0) else ps_mm2
                mk.waitop("tensor", ss)
                for k in range(8):
                    mk.op("tensor", "matmul", ps[:, :],
                          b("wbuf")[:, (mc - g0) * 1024 + k * 128:
                                    (mc - g0) * 1024 + (k + 1) * 128],
                          b("fmq")[:, k * 512:(k + 1) * 512],
                          start=(k == 0), stop=(k == 7),
                          inc=(sp if k == 7 else None))
                if mc < 16:
                    mk.op("scalar", "copy", b("zf")[:, mc * 512:(mc + 1) * 512],
                          ps[:, :], wait=sp, inc=ss)
                elif mc < 33:
                    c = mc - 16
                    mk.op("scalar", "copy",
                          b("xbc")[:, c * 515 + 3: c * 515 + 515],
                          ps[:, :], wait=sp, inc=ss)
                else:
                    mk.op("scalar", "copy", b("sc_big")[0:32, 512:1024], ps[0:32, :],
                          wait=sp, inc=ss)
        t32 = b("sc_sh")
        mk.op("scalar", "activation", b("dtq")[:, :], b("sc_big")[0:32, 512:1024], AF.Relu,
              wait=ss, bias=b("dtbias_s")[:, :])
        mk.op("scalar", "activation", t32[0:32, :], b("sc_big")[0:32, 512:1024], AF.Abs,
              bias=b("dtbias_s")[:, :])
        mk.op("scalar", "activation", t32[0:32, :], t32[0:32, :], AF.Exp, scale=-1.0)
        mk.op("scalar", "activation", t32[0:32, :], t32[0:32, :], AF.Ln, bias=b("c_one")[0:32, :],
              inc=ss)
        mk.op("vector", "tensor_tensor", b("dtq")[:, :], b("dtq")[:, :],
              t32[0:32, :], wait=ss, op=OP.add, inc=sv)

        if self.dbg and q == 0:
            self.dump("zf", b("zf")[:, :], (128, 8192), BF16)
            self.dump("xbc", b("xbc")[:, :], (128, 8755), BF16)
            self.dump("dtq", b("dtq")[:, :], (32, 512), F32)
        for ch in range(17):
            base = ch * 515
            acc = b("sc_big")[:, 0:512]
            mk.op("vector", "tensor_scalar", acc,
                  b("xbc")[:, base:base + 512],
                  b("convw_s")[:, ch * 4: ch * 4 + 1], None, wait=ss, op0=OP.mult)
            for kk in range(1, 4):
                mk.op("vector", "scalar_tensor_tensor", acc,
                      b("xbc")[:, base + kk: base + kk + 512],
                      b("convw_s")[:, ch * 4 + kk: ch * 4 + kk + 1], acc,
                      op0=OP.mult, op1=OP.add,
                      inc=(sv if kk == 3 else None))
            dest = (b("xf")[:, ch * 512:(ch + 1) * 512] if ch < 16
                    else b("bcu")[:, :])
            mk.op("scalar", "activation", dest, acc, AF.Silu, wait=sv,
                  bias=b("convb_s")[:, ch:ch + 1], inc=ss)
        mk.op("vector", "tensor_copy", b("c0")[:, :], b("bcu")[64:128, :],
              wait=ss, inc=sv)
        for ch in range(17):
            base = ch * 515
            mk.op("vector", "tensor_copy", b("xbc")[:, base:base + 3],
                  b("xbc")[:, base + 512:base + 515],
                  inc=(sv if ch == 16 else None))

        if self.dbg and q == 0:
            self.dump("xf", b("xf")[:, :], (128, 8192), BF16)
            self.dump("bcu", b("bcu")[:, :], (128, 512), BF16)
        for cl in range(4):
            s0 = cl * 128
            mk.op("vector", "tensor_tensor_scan", b("dcum")[:, s0:s0 + 128],
                  b("ones_f")[0:32, 0:128], b("dtq")[:, s0:s0 + 128], 0.0,
                  op0=OP.mult, op1=OP.add, inc=sv)
            mk.op("scalar", "activation", b("w2r")[0:32, s0:s0 + 128],
                  b("dcum")[:, s0:s0 + 128], AF.Exp, wait=sv,
                  scale=b("acol_s")[:, :], inc=ss)
            mk.op("tensor", "transpose", ps_trf[:, 0:32], b("dtq")[:, s0:s0 + 128],
                  b("ident_f_s")[0:32, 0:32], wait=sv, inc=sp)
            mk.op("vector", "tensor_copy", b("dtT")[:, cl * 32:(cl + 1) * 32],
                  ps_trf[:, 0:32], wait=sp, inc=sv)
            mk.op("tensor", "transpose", ps_trf[:, 32:64], b("dcum")[:, s0:s0 + 128],
                  b("ident_f_s")[0:32, 0:32], wait=sv, inc=sp)
            mk.op("vector", "tensor_copy", b("DtT")[:, cl * 32:(cl + 1) * 32],
                  ps_trf[:, 32:64], wait=sp, inc=sv)
            sf = b("sc_sh")
            mk.op("vector", "tensor_scalar", sf[0:32, 0:128],
                  b("dcum")[:, s0:s0 + 128],
                  b("dcum")[:, s0 + 127:s0 + 128], -1.0,
                  op0=OP.subtract, op1=OP.mult, inc=sv)
            mk.op("scalar", "activation", sf[0:32, 0:128], sf[0:32, 0:128], AF.Exp,
                  wait=sv, scale=b("acol_s")[:, :], inc=ss)
            mk.op("vector", "tensor_tensor", sf[0:32, 0:128], sf[0:32, 0:128],
                  b("dtq")[:, s0:s0 + 128], wait=ss, op=OP.mult, inc=sv)
            mk.op("tensor", "transpose", ps_trf[:, 64:96], sf[0:32, 0:128],
                  b("ident_f_s")[0:32, 0:32], wait=sv, inc=sp)
            mk.op("vector", "tensor_copy", b("stm")[:, cl * 32:(cl + 1) * 32],
                  ps_trf[:, 64:96], wait=sp, inc=sv)
            mk.op("scalar", "activation", b("sc_small")[0:32, 0:1],
                  b("dcum")[:, s0 + 127:s0 + 128], AF.Exp,
                  scale=b("acol_s")[:, :], inc=ss)
            mk.op("tensor", "transpose", ps_trf[0:1, 96:128],
                  b("sc_small")[0:32, 0:1], b("ident_f_s")[0:32, 0:32], wait=ss, inc=sp)
            mk.op("vector", "tensor_copy", b("sc_small")[0:1, 32:64],
                  ps_trf[0:1, 96:128], wait=sp, inc=sv)
            mk.op("tensor", "matmul", ps_row[:, 0:32], b("ones_f")[0:1, :],
                  b("sc_small")[0:1, 32:64], wait=sv, start=True, stop=True, inc=sp)
            mk.op("vector", "tensor_copy", b("dbc")[:, cl * 32:(cl + 1) * 32],
                  ps_row[:, 0:32], wait=sp, inc=sv)
            for dk in range(16):
                mk.op("tensor", "transpose", ps_tr[:, 128:256],
                      b("xf")[:, dk * 512 + s0: dk * 512 + s0 + 128],
                      b("ident_bf_s")[:, :], wait=sv, wait2=ss, inc=sp)
                mk.op("scalar", "copy",
                      b("xtm")[:, cl * 2048 + dk * 128: cl * 2048 + (dk + 1) * 128],
                      ps_tr[:, 128:256], wait=sp, inc=ss)
            mk.op("tensor", "transpose", ps_tr[:, 256:320],
                  b("bcu")[0:64, s0:s0 + 128], b("ident_bf_s")[0:64, 0:64],
                  wait=sv, wait2=ss, inc=sp)
            mk.op("scalar", "copy", b("btm")[:, cl * 64:(cl + 1) * 64],
                  ps_tr[:, 256:320], wait=sp, inc=ss)
            mk.op("tensor", "transpose", ps_trf[:, 0:32],
                  b("w2r")[0:32, s0:s0 + 128], b("ident_f_s")[0:32, 0:32],
                  wait=sv, wait2=ss, inc=sp)
            mk.op("vector", "tensor_copy", b("w2t")[:, cl * 32:(cl + 1) * 32],
                  ps_trf[:, 0:32], wait=sp, inc=sv)

        for cl in range(4):
            s0 = cl * 128
            cc = cl * 32
            mk.waitop("tensor", sv)
            mk.op("tensor", "matmul", ps_g[:, 0:128], b("bcu")[0:64, s0:s0 + 128],
                  b("c0")[0:64, s0:s0 + 128], start=True, stop=True, inc=sp)
            mk.op("scalar", "copy", b("sc_g")[:, :], ps_g[:, 0:128], wait=sp,
                  wait2=sv, inc=ss)
            for h in range(32):
                col = cc + h
                mk.op("vector", "tensor_scalar", b("sc_cb")[:, :],
                      b("ones_f")[:, :], b("DtT")[:, col:col + 1], None,
                      wait=sp, op0=OP.mult, inc=sv)
                mk.op("tensor", "transpose", ps_row[:, 128:256], b("sc_cb")[:, :],
                      b("ident_f_s")[:, :], wait=sv, inc=sp)
                mk.op("vector", "scalar_tensor_tensor", b("sc_a")[:, :],
                      ps_row[:, 128:256], b("DtT")[:, col:col + 1],
                      b("cmask_s")[:, :], wait=sp, op0=OP.subtract, op1=OP.add,
                      inc=sv)
                mk.op("scalar", "activation", b("sc_l")[:, :], b("sc_a")[:, :],
                      AF.Exp, wait=sv, scale=b("abc_s")[:, h:h + 1], inc=ss)
                mk.op("vector", "scalar_tensor_tensor", b("sc_m")[:, :],
                      b("sc_g")[:, :], b("dtT")[:, col:col + 1], b("sc_l")[:, :],
                      wait=ss, op0=OP.mult, op1=OP.mult, inc=sv)
                mk.op("vector", "tensor_scalar", b("sc_cb2")[:, :],
                      b("ones_f")[:, 0:64], b("w2t")[:, col:col + 1], None,
                      op0=OP.mult, inc=sv)
                mk.op("tensor", "transpose", ps_trf[0:64, 128:256],
                      b("sc_cb2")[:, :], b("ident_f_s")[:, :], wait=sv, inc=sp)
                mk.op("vector", "tensor_tensor", b("sc_cw")[:, :],
                      b("c0")[:, s0:s0 + 128], ps_trf[0:64, 128:256],
                      wait=sp, op=OP.mult, inc=sv)
                xtm_h = b("xtm")[:, cl * 2048 + h * 64: cl * 2048 + (h + 1) * 64]
                mk.op("tensor", "matmul", ps_y[0:64, 0:128], xtm_h, b("sc_m")[:, :],
                      wait=sv, wait2=ss, start=True, stop=False)
                mk.op("tensor", "matmul", ps_y[0:64, 0:128],
                      b("hrun_bf")[0:64, h * 64:(h + 1) * 64], b("sc_cw")[:, :],
                      start=False, stop=True, inc=sp)
                mk.op("vector", "tensor_scalar", b("sc_bsc")[:, :],
                      b("btm")[:, cl * 64:(cl + 1) * 64], b("stm")[:, col:col + 1],
                      None, op0=OP.mult, inc=sv)
                mk.op("tensor", "matmul", ps_h[0:64, 0:64], b("sc_bsc")[:, :],
                      xtm_h, wait=sv, start=True, stop=True, inc=sp)
                mk.op("scalar", "copy",
                      b("yasm")[(h % 2) * 64:(h % 2) * 64 + 64,
                                (h // 2) * 128:(h // 2 + 1) * 128],
                      ps_y[0:64, 0:128], wait=sp,
                      wait2=(sv if h == 0 else None), inc=ss)
                mk.op("vector", "scalar_tensor_tensor",
                      b("hrun")[0:64, h * 64:(h + 1) * 64],
                      b("hrun")[0:64, h * 64:(h + 1) * 64],
                      b("dbc")[0:64, col:col + 1], ps_h[0:64, 0:64],
                      wait=sp, op0=OP.mult, op1=OP.add)
                mk.op("vector", "tensor_copy",
                      b("hrun_bf")[0:64, h * 64:(h + 1) * 64],
                      b("hrun")[0:64, h * 64:(h + 1) * 64], inc=sv)
            if self.dbg and q == 0 and cl == 0:
                self.dump("yasm0", b("yasm")[:, :], (128, 2048), F32)
                self.dump("hrun0", b("hrun")[:, :], (64, 2048), F32)
                self.dump("dtT", b("dtT")[:, :], (128, 128), F32)
                self.dump("DtT", b("DtT")[:, :], (128, 128), F32)
                self.dump("stm", b("stm")[:, :], (128, 128), F32)
                self.dump("dbc", b("dbc")[:, :], (128, 128), F32)
                self.dump("w2r", b("w2r")[:, :], (64, 512), F32)
                self.dump("abc", b("abc_s")[:, :], (128, 32), F32)
                self.dump("acol", b("acol_s")[:, :], (32, 1), F32)
                self.dump("sc_g0", b("sc_g")[:, :], (128, 128), F32)
                self.dump("sc_m0", b("sc_m")[:, :], (128, 128), BF16)
                self.dump("btm0", b("btm")[:, :], (128, 256), BF16)
            for j in range(16):
                jy = b("yasm")[:, j * 128:(j + 1) * 128]
                mk.op("vector", "scalar_tensor_tensor", jy,
                      b("xf")[:, j * 512 + s0: j * 512 + s0 + 128],
                      b("d128_s")[:, j:j + 1], jy, wait=ss, op0=OP.mult,
                      op1=OP.add, inc=sv)
                mk.op("scalar", "activation", b("sc_sz")[:, :],
                      b("zf")[:, j * 512 + s0: j * 512 + s0 + 128], AF.Silu,
                      wait=sv, inc=ss)
                mk.op("vector", "tensor_tensor", jy, jy, b("sc_sz")[:, :],
                      wait=ss, op=OP.mult, inc=sv)
                mk.op("scalar", "activation", b("sc_big")[:, 0:128], jy,
                      AF.Square, wait=sv, wait2=sg, inc=ss)
                mk.op("gpsimd", "tensor_reduce", b("ssrow")[0:1, 0:128],
                      b("sc_big")[:, 0:128], wait=ss, axis=AX.C, op=OP.add)
                if j == 0:
                    mk.op("gpsimd", "tensor_copy", b("ssacc")[0:1, :],
                          b("ssrow")[0:1, 0:128], wait=sp, inc=sg)
                else:
                    mk.op("gpsimd", "tensor_tensor", b("ssacc")[0:1, :],
                          b("ssacc")[0:1, :], b("ssrow")[0:1, 0:128],
                          op=OP.add, inc=sg)
            mk.op("scalar", "activation", b("ssacc")[0:1, :], b("ssacc")[0:1, :],
                  AF.Ln, wait=sg, scale=1.0 / 2048, bias=b("c_eps")[0:1, :])
            mk.op("scalar", "activation", b("ssacc")[0:1, :], b("ssacc")[0:1, :],
                  AF.Exp, scale=-0.5, inc=ss)
            mk.op("tensor", "matmul", ps_row[:, 384:512], b("ones_f")[0:1, :],
                  b("ssacc")[0:1, :], wait=ss, wait2=sv, start=True, stop=True,
                  inc=sp)
            for j in range(16):
                jy = b("yasm")[:, j * 128:(j + 1) * 128]
                mk.op("vector", "scalar_tensor_tensor",
                      b("ygf")[:, j * 512 + s0: j * 512 + s0 + 128],
                      jy, b("normw_s")[:, j:j + 1], ps_row[:, 384:512],
                      wait=sp, op0=OP.mult, op1=OP.mult,
                      inc=(sv if j == 15 else None))
            mk.waitop("tensor", sv)

        if self.dbg and q == 0:
            self.dump("ygf", b("ygf")[:, :], (128, 8192), BF16)
            self.dump("dcum", b("dcum")[:, :], (32, 512), F32)
            self.dump("xtm", b("xtm")[:, :], (128, 8192), BF16)
        woutp = b("l0_w_outproj")
        for g0 in range(0, 8, 4):
            mk.waitop("gpsimd", sp)
            for mc in range(g0, g0 + 4):
                mk.dma("gpsimd", b("wbuf")[:, (mc - g0) * 2048:(mc - g0 + 1) * 2048],
                       woutp.ap()[mc, :, :], lw)
            mk.waitop("tensor", lw)
            for mc in range(g0, g0 + 4):
                ps = ps_mm if (mc % 2 == 0) else ps_mm2
                mk.waitop("tensor", ss)
                for k in range(16):
                    mk.op("tensor", "matmul", ps[:, :],
                          b("wbuf")[:, (mc - g0) * 2048 + k * 128:
                                    (mc - g0) * 2048 + (k + 1) * 128],
                          b("ygf")[:, k * 512:(k + 1) * 512],
                          start=(k == 0), stop=(k == 15),
                          inc=(sp if k == 15 else None))
                mk.op("scalar", "copy", b("hf")[:, mc * 512:(mc + 1) * 512],
                      ps[:, :], wait=sp, inc=ss)

        mk.waitop("gpsimd", sp)
        mk.dma("gpsimd", b("wbuf2")[:, :].rearrange("p (k n) -> p k n", k=8),
               b("l0_w_toh").ap().rearrange("(k p) n -> p k n", p=128), lw)
        for tc in range(4):
            mk.dma("sync", b("sc_big")[:, :],
                   x_hbm[r0 + tc * 128: r0 + (tc + 1) * 128, :], ld,
                   wait=sv, wait2=ss)
            mk.waitop("tensor", lw)
            for n in range(2):
                ps = ps_mm if (n == 0) else ps_mm2
                mk.waitop("tensor", sv)
                for k in range(8):
                    mk.op("tensor", "matmul", ps[:, :],
                          b("hf")[:, k * 512 + tc * 128: k * 512 + (tc + 1) * 128],
                          b("wbuf2")[:, k * 1024 + n * 512: k * 1024 + (n + 1) * 512],
                          start=(k == 0), stop=(k == 7),
                          inc=(sp if k == 7 else None))
                mk.waitop("vector", ld)
                mk.op("vector", "tensor_tensor",
                      b("x1q")[:, tc * 1024 + n * 512: tc * 1024 + (n + 1) * 512],
                      ps[:, :], b("sc_big")[:, n * 512:(n + 1) * 512],
                      wait=sp, op=OP.add, inc=sv)
            mk.dma("sync", x1_hbm[r0 + tc * 128: r0 + (tc + 1) * 128, :],
                   b("x1q")[:, tc * 1024:(tc + 1) * 1024], so, wait=sv)

        if self.dbg and q == 0:
            self.dump("hf", b("hf")[:, :], (128, 4096), BF16)
        for tc in range(4):
            sl = b("x1q")[:, tc * 1024:(tc + 1) * 1024]
            for phase in range(2):
                mk.ln_stats(sl, b("stat_nm")[:, tc:tc + 1],
                            b("stat_rs")[:, tc:tc + 1], 1024, sv, ss,
                            wait=(so if phase == 0 else None))
                if phase == 0:
                    mk.op("vector", "tensor_scalar", sl, sl,
                          b("stat_nm")[:, tc:tc + 1], b("stat_rs")[:, tc:tc + 1],
                          wait=ss, op0=OP.add, op1=OP.mult, inc=sv)
                else:
                    mk.op("vector", "tensor_scalar",
                          b("tmq")[:, tc * 1024:(tc + 1) * 1024], sl,
                          b("stat_nm")[:, tc:tc + 1], b("stat_rs")[:, tc:tc + 1],
                          wait=ss, op0=OP.add, op1=OP.mult, inc=sv)
        for tc in range(4):
            for k in range(8):
                mk.op("tensor", "transpose", ps_tr[:, 0:128],
                      b("tmq")[:, tc * 1024 + k * 128: tc * 1024 + (k + 1) * 128],
                      b("ident_bf_s")[:, :], wait=sv, wait2=ss, inc=sp)
                mk.op("scalar", "copy",
                      b("fmq")[:, k * 512 + tc * 128: k * 512 + (tc + 1) * 128],
                      ps_tr[:, 0:128], wait=sp, inc=ss)
        for k in range(8):
            mk.op("vector", "tensor_scalar", b("fmq")[:, k * 512:(k + 1) * 512],
                  b("fmq")[:, k * 512:(k + 1) * 512], b("ln1w_s")[:, k:k + 1],
                  b("ln1b_s")[:, k:k + 1], wait=ss, op0=OP.mult, op1=OP.add,
                  inc=sv)

        wq = b("l0f_w_qkv")
        mk.waitop("scalar", so)
        for n6 in range(6):
            half = (n6 % 2) * 4096
            mk.waitop("gpsimd", sp)
            mk.dma("gpsimd",
                   b("wbuf2")[:, half:half + 4096].rearrange("p (k n) -> p k n", k=8),
                   wq.ap()[:, n6 * 512:(n6 + 1) * 512]
                   .rearrange("(k p) n -> p k n", p=128), lw)
            mk.waitop("tensor", lw)
            for tc in range(4):
                ps = ps_mm if (tc % 2 == 0) else ps_mm2
                mk.waitop("tensor", ss)
                for k in range(8):
                    mk.op("tensor", "matmul", ps[:, :],
                          b("fmq")[:, k * 512 + tc * 128: k * 512 + (tc + 1) * 128],
                          b("wbuf2")[:, half + k * 512: half + (k + 1) * 512],
                          wait=(sv if k == 0 else None),
                          start=(k == 0), stop=False)
                mk.op("tensor", "matmul", ps[:, :], b("ones_bf")[0:1, :],
                      b("qkvb_s")[0:1, n6 * 512:(n6 + 1) * 512],
                      start=False, stop=True, inc=sp)
                mk.op("scalar", "copy", b("kvstage")[:, :], ps[:, :],
                      wait=sp, wait2=so, inc=ss)
                rr = r0 + tc * 128
                if n6 < 2:
                    dst = q_hbm[rr:rr + 128, n6 * 512:(n6 + 1) * 512]
                else:
                    dst = kv_hbm[rr:rr + 128, (n6 - 2) * 512:(n6 - 1) * 512]
                mk.dma("sync", dst, b("kvstage")[:, :], so, wait=ss)

    def emit(self, x_hbm, x1_hbm, q_hbm, kv_hbm):
        self.alloc()
        self.mk.sbuf("hrun_bf", (64, 32 * 64), BF16)
        self.mk.sbuf("kvstage", (128, 512), BF16)
        self.load_consts()
        for q in range(4):
            self.quarter(q, x_hbm, x1_hbm, q_hbm, kv_hbm)




def emit_top2_gates(mk, glog, gout, scr, renorm, gaps=False):
    b = OP
    sv = mk.sem("sv")
    ss = mk.sem("ss")
    g3 = glog[:, :].rearrange("p (t e) -> p t e", e=8)
    gsh = scr[:, 0:128]
    gsh3 = gsh.rearrange("p (t e) -> p t e", e=8)
    bA = scr[:, 128:256]
    bA3 = bA.rearrange("p (t e) -> p t e", e=8)
    knock = scr[:, 256:384]
    knock3 = knock.rearrange("p (t e) -> p t e", e=8)
    mask = scr[:, 384:512]
    mask3 = mask.rearrange("p (t e) -> p t e", e=8)
    def s16(i):
        return scr[:, 512 + i * 16: 512 + (i + 1) * 16]
    def bc(t16):
        return t16.unsqueeze(2).broadcast_to([128, 16, 8])
    m1, m1b, s_, srec, mm, mmb, m2, m2b, s2, s2b, den, sb = [s16(i) for i in range(12)]

    def vop(*a, **kw):
        if gaps:
            mk.gwait("vector", kw.pop("wait"))
        mk.op("vector", *a, **kw)

    def sop(*a, **kw):
        if gaps and kw.get("wait") is not None:
            mk.gwait("scalar", kw.pop("wait"))
        mk.op("scalar", *a, **kw)

    vop("tensor_reduce", m1, g3, wait=ss, axis=AX.X, op=OP.max, inc=sv)
    sop("copy", m1b, m1, wait=sv, inc=ss)
    vop("tensor_tensor", gsh3, g3, bc(m1b), wait=ss, op=OP.subtract, inc=sv)
    sop("activation", gsh, gsh, AF.Exp, wait=sv, inc=ss)
    vop("tensor_reduce", s_, gsh3, wait=ss, axis=AX.X, op=OP.add, inc=sv)
    sop("copy", sb, s_, wait=sv)
    sop("activation", srec, s_, AF.Ln)
    sop("activation", srec, srec, AF.Exp, scale=-1.0, inc=ss)
    vop("tensor_reduce", mm, gsh3, wait=ss, axis=AX.X, op=OP.max, inc=sv)
    sop("copy", mmb, mm, wait=sv, inc=ss)
    vop("tensor_tensor", mask3, gsh3, bc(mmb), wait=ss, op=OP.is_ge, inc=sv)
    sop("copy", bA, mask, wait=sv, inc=ss)
    vop("scalar_tensor_tensor", knock, bA, -MASK_BIG, gsh,
        wait=ss, op0=OP.mult, op1=OP.add, inc=sv)
    sop("copy", bA, knock, wait=sv, inc=ss)
    vop("tensor_reduce", m2, bA3, wait=ss, axis=AX.X, op=OP.max, inc=sv)
    sop("copy", m2b, m2, wait=sv, inc=ss)
    vop("tensor_tensor", mask3, gsh3, bc(m2b), wait=ss, op=OP.is_ge, inc=sv)
    sop("copy", bA, mask, wait=sv, inc=ss)
    vop("tensor_tensor", knock, gsh, bA, wait=ss, op=OP.mult,
        inc=sv)
    sop("copy", bA, knock, wait=sv, inc=ss)
    if not renorm:
        vop("tensor_tensor", gout[:, :].rearrange("p (t e) -> p t e", e=8),
            bA3, bc(srec), wait=ss, op=OP.mult, inc=sv)
    else:
        vop("tensor_reduce", s2, bA3, wait=ss, axis=AX.X, op=OP.add, inc=sv)
        sop("copy", s2b, s2, wait=sv, inc=ss)
        vop("scalar_tensor_tensor", den, sb, 1e-8, s2b, wait=ss,
            op0=OP.mult, op1=OP.add, inc=sv)
        sop("activation", den, den, AF.Ln, wait=sv)
        sop("activation", den, den, AF.Exp, scale=-1.0, inc=ss)
        vop("tensor_tensor", gout[:, :].rearrange("p (t e) -> p t e", e=8),
            bA3, bc(den), wait=ss, op=OP.mult, inc=sv)



class L0Ffn:

    @staticmethod
    def prep_inputs(inp):
        out = {}
        out["l0f_w_attnout"] = rhs_weight(inp["L0_ffn_out_proj_w"].T)
        out["l0f_attnob"] = inp["L0_ffn_out_proj_b"].reshape(1, 1024).astype(np.float32)
        out["l0f_ln2w"] = fm_cols(inp["L0_ffn_ln2_w"])
        out["l0f_ln2b"] = fm_cols(inp["L0_ffn_ln2_b"])
        out["l0f_w_router"] = rhs_weight(inp["L0_ffn_router_w"].T)
        w1 = inp["L0_ffn_w1"]
        out["l0f_w1"] = np.stack([tile_lhsT(w1[e]) for e in range(8)])
        w2 = inp["L0_ffn_w2"]
        out["l0f_w2"] = np.stack([rhs_weight(w2[e]) for e in range(8)])
        return out

    def __init__(self, mk: MK, dbg=False, moe=True):
        self.mk = mk
        self.dbg = dbg
        self.moe = moe

    def dump(self, name, ap, shape, dtype):
        mk = self.mk
        h = mk.dram_out("dbg2_" + name, shape, dtype)
        for s in ("sv", "ss", "sp", "sg"):
            mk.waitop("sync", mk.sem(s))
        mk.dma("sync", h[tuple(slice(0, n) for n in shape)], ap, mk.sem("so"))

    def alloc(self):
        mk = self.mk
        sb = mk.sbuf
        sb("ident_bf_s", (128, 128), BF16)
        sb("ones_f", (128, 128), F32)
        sb("ones_bf", (128, 128), BF16)
        sb("c_eps", (128, 1), F32)
        sb("x1i", (128, 16 * 1024), F32)
        sb("hn2", (128, 8 * 2048), BF16)
        sb("gates", (128, 16 * 8), F32)
        sb("glog", (128, 16 * 8), F32)
        sb("gscr", (128, 704), F32)
        sb("rmask_bc", (128, 8), F32)
        sb("aob_s", (1, 1024), BF16)
        sb("ln2w_s", (128, 8), F32)
        sb("ln2b_s", (128, 8), F32)
        sb("qtile", (128, 1024), BF16)
        sb("kvt0", (128, 2048), BF16)
        sb("kvt1", (128, 2048), BF16)
        sb("prod", (128, 1024), F32)
        sb("score_s", (128, 128), F32)
        sb("score2", (128, 128), F32)
        sb("attn_acc", (128, 1024), F32)
        sb("attn_bf", (128, 1024), BF16)
        sb("xftile", (128, 1024), F32)
        sb("tmp_tm", (128, 1024), BF16)
        sb("wbuf", (128, 16 * 1024), BF16)
        sb("wbuf2", (128, 16 * 1024), BF16)
        sb("hmid", (128, 16 * 256), BF16)
        sb("stat_nm", (128, 16), F32)
        sb("stat_rs", (128, 16), F32)
        sb("sc_small", (128, 32), F32)
        mk.psum("ps_mm", (128, 512), F32)
        mk.psum("ps_mm2", (128, 512), F32)
        mk.psum("ps_tr", (128, 1024), BF16)
        mk.psum("ps_r", (128, 512), F32)

    def emit(self, x1_hbm, q_hbm, kv_hbm, x2_hbm, poison=False):
        mk = self.mk
        b = mk.buf
        self.alloc()
        if poison:
            mk.poison_all()
        ld = mk.sem("ld")
        lw = mk.sem("lw")
        so = mk.sem("so")
        sv = mk.sem("sv")
        ss = mk.sem("ss")
        sp = mk.sem("sp")
        sg = mk.sem("sg")
        ps_mm, ps_mm2, ps_tr, ps_r = b("ps_mm"), b("ps_mm2"), b("ps_tr"), b("ps_r")

        for nm, dst in [("ident_bf", "ident_bf_s"), ("l0f_ln2w", "ln2w_s"),
                        ("l0f_ln2b", "ln2b_s")]:
            mk.dma("sync", b(dst)[:, :], b(nm).ap(), ld)
        mk.dma("gpsimd", b("aob_s")[:, :], b("l0f_attnob").ap(), lw)
        mk.op("gpsimd", "memset", b("ones_f")[:, :], 1.0, inc=sg)
        mk.op("gpsimd", "memset", b("ones_bf")[:, :], 1.0, inc=sg)
        mk.op("gpsimd", "memset", b("c_eps")[:, :], LN_EPS, inc=sg)
        mk.sbuf("rmrow", (1, 8), F32)
        mk.dma("sync", b("rmrow")[:, :], b("rmask").ap(), ld)
        mk.waitop("tensor", ld)
        mk.waitop("tensor", sg)
        mk.op("tensor", "matmul", ps_r[:, 0:8], b("ones_f")[0:1, :],
              b("rmrow")[0:1, :], start=True, stop=True, inc=sp)
        mk.op("vector", "tensor_copy", b("rmask_bc")[:, :], ps_r[:, 0:8],
              wait=sp, inc=sv)
        for e in ("vector", "scalar"):
            mk.waitop(e, ld)
            mk.waitop(e, lw)
            mk.waitop(e, sg)

        mk.dma("gpsimd", b("wbuf")[:, 0:8192].rearrange("p (k n) -> p k n", k=8),
               b("l0f_w_attnout").ap().rearrange("(k p) n -> p k n", p=128), lw)
        mk.dma("gpsimd", b("wbuf")[:, 8192:8256].rearrange("p (k n) -> p k n", k=8),
               b("l0f_w_router").ap().rearrange("(k p) n -> p k n", p=128), lw)

        for tc in range(16):
            r0 = tc * 128
            mk.dma("sync", b("xftile")[:, :], x1_hbm[r0:r0 + 128, :], ld,
                   wait=sv, wait2=ss)
            mk.dma("sync", b("qtile")[:, :], q_hbm[r0:r0 + 128, :], ld, wait=sv)
            mk.waitop("vector", ld)
            mk.ln_stats(b("xftile"), b("stat_nm")[:, 0:1], b("stat_rs")[:, 0:1],
                        1024, sv, ss, gaps=True)
            mk.gwait("vector", ss)
            mk.op("vector", "tensor_scalar", b("xftile")[:, :], b("xftile")[:, :],
                  b("stat_nm")[:, 0:1], b("stat_rs")[:, 0:1],
                  op0=OP.add, op1=OP.mult, inc=sv)
            stg = b("score_s")
            for r in range(8):
                kvt = b("kvt0") if r % 2 == 0 else b("kvt1")
                mk.dma("sync", kvt[:, :], kv_hbm[r, r0:r0 + 128, :], ld,
                       wait=(sv if r >= 2 else None))
                mk.waitop("vector", ld)
                mk.op("vector", "tensor_tensor", b("prod")[:, :], b("qtile")[:, :],
                      kvt[:, 0:1024], op=OP.mult)
                mk.op("vector", "tensor_reduce", stg[:, r * 16:(r + 1) * 16],
                      b("prod")[:, :].rearrange("p (n d) -> p n d", d=64),
                      axis=AX.X, op=OP.add, inc=sv)
            sc_all = b("score2")
            sc3 = sc_all[:, :].rearrange("p (n r) -> p n r", r=8)
            mk.gwait("scalar", sv)
            mk.op("scalar", "copy", b("prod")[:, 0:128], stg[:, :], inc=ss)
            rm3 = b("rmask_bc")[:, :].unsqueeze(1).broadcast_to([128, 16, 8])
            mk.gwait("vector", ss)
            mk.op("vector", "scalar_tensor_tensor", sc3,
                  b("prod")[:, 0:128].rearrange("p (r n) -> p n r", r=8),
                  0.125, rm3, op0=OP.mult, op1=OP.add, inc=sv)
            mk.gwait("scalar", sv)
            mk.op("scalar", "copy", b("prod")[:, 128:256], sc_all[:, :],
                  inc=ss)
            mx = b("sc_small")[:, 0:16]
            mk.gwait("vector", ss)
            mk.op("vector", "tensor_reduce", mx,
                  b("prod")[:, 128:256].rearrange("p (n r) -> p n r", r=8),
                  axis=AX.X, op=OP.max, inc=sv)
            mk.gwait("scalar", sv)
            mk.op("scalar", "copy", b("sc_small")[:, 16:32].bitcast(F32),
                  mx, inc=ss)
            mk.gwait("vector", ss)
            mk.op("vector", "tensor_tensor", sc3,
                  b("prod")[:, 128:256].rearrange("p (n r) -> p n r", r=8),
                  b("sc_small")[:, 16:32].unsqueeze(2).broadcast_to([128, 16, 8]),
                  op=OP.subtract, inc=sv)
            mk.gwait("scalar", sv)
            mk.op("scalar", "activation", sc_all[:, :], sc_all[:, :], AF.Exp,
                  inc=ss)
            mk.gwait("vector", ss)
            mk.op("vector", "tensor_reduce", mx, sc3, axis=AX.X,
                  op=OP.add, inc=sv)
            mk.gwait("scalar", sv)
            mk.op("scalar", "activation", b("sc_small")[:, 16:32].bitcast(F32),
                  mx, AF.Ln)
            mk.op("scalar", "activation", b("sc_small")[:, 16:32].bitcast(F32),
                  b("sc_small")[:, 16:32].bitcast(F32), AF.Exp, scale=-1.0,
                  inc=ss)
            for r in range(8):
                kvt = b("kvt0") if r % 2 == 0 else b("kvt1")
                mk.dma("sync", kvt[:, :], kv_hbm[r, r0:r0 + 128, :], ld, wait=sv)
                mk.waitop("vector", ld)
                pb = sc3[:, :, r].unsqueeze(2).broadcast_to([128, 16, 64])
                v3 = kvt[:, 1024:2048].rearrange("p (n d) -> p n d", d=64)
                if r == 0:
                    mk.gwait("vector", ss)
                    mk.op("vector", "tensor_tensor",
                          b("attn_acc")[:, :].rearrange("p (n d) -> p n d", d=64),
                          v3, pb, op=OP.mult, inc=sv)
                else:
                    mk.op("vector", "tensor_tensor",
                          b("prod")[:, :].rearrange("p (n d) -> p n d", d=64),
                          v3, pb, op=OP.mult)
                    mk.op("vector", "tensor_tensor", b("attn_acc")[:, :],
                          b("attn_acc")[:, :], b("prod")[:, :],
                          op=OP.add, inc=sv)
            mk.op("vector", "tensor_tensor",
                  b("attn_bf")[:, :].rearrange("p (n d) -> p n d", d=64),
                  b("attn_acc")[:, :].rearrange("p (n d) -> p n d", d=64),
                  b("sc_small")[:, 16:32].unsqueeze(2).broadcast_to([128, 16, 64]),
                  op=OP.mult, inc=sv)
            for k in range(8):
                mk.op("tensor", "transpose", ps_tr[:, 0:128],
                      b("attn_bf")[:, k * 128:(k + 1) * 128],
                      b("ident_bf_s")[:, :], wait=sv, wait2=ss, inc=sp)
                mk.op("scalar", "copy",
                      b("tmp_tm")[:, k * 128:(k + 1) * 128], ps_tr[:, 0:128],
                      wait=sp, inc=ss)
            for n in range(2):
                ps = ps_mm if n == 0 else ps_mm2
                mk.waitop("tensor", sv)
                mk.waitop("tensor", ss)
                for k in range(8):
                    mk.op("tensor", "matmul", ps[:, :],
                          b("tmp_tm")[:, k * 128:(k + 1) * 128],
                          b("wbuf")[:, k * 1024 + n * 512: k * 1024 + (n + 1) * 512],
                          wait=(lw if tc == 0 and n == 0 and k == 0 else None),
                          start=(k == 0), stop=False)
                mk.op("tensor", "matmul", ps[:, :], b("ones_bf")[0:1, :],
                      b("aob_s")[0:1, n * 512:(n + 1) * 512],
                      start=False, stop=True, inc=sp)
                mk.op("vector", "tensor_tensor",
                      b("x1i")[:, tc * 1024 + n * 512: tc * 1024 + (n + 1) * 512],
                      ps[:, :], b("xftile")[:, n * 512:(n + 1) * 512],
                      wait=sp, op=OP.add, inc=sv)
            x1i_t = b("x1i")[:, tc * 1024:(tc + 1) * 1024]
            mk.ln_stats(x1i_t, b("stat_nm")[:, 1:2], b("stat_rs")[:, 1:2],
                        1024, sv, ss, gaps=True)
            mk.gwait("vector", ss)
            mk.op("vector", "tensor_scalar", b("tmp_tm")[:, :], x1i_t,
                  b("stat_nm")[:, 1:2], b("stat_rs")[:, 1:2],
                  op0=OP.add, op1=OP.mult, inc=sv)
            for k in range(8):
                mk.op("tensor", "transpose", ps_tr[:, 0:128],
                      b("tmp_tm")[:, k * 128:(k + 1) * 128],
                      b("ident_bf_s")[:, :], wait=sv, wait2=ss, inc=sp)
                mk.op("scalar", "copy",
                      b("hn2")[:, k * 2048 + tc * 128: k * 2048 + (tc + 1) * 128],
                      ps_tr[:, 0:128], wait=sp, inc=ss)
            mk.gwait("vector", ss)
            for k in range(8):
                sl = b("hn2")[:, k * 2048 + tc * 128: k * 2048 + (tc + 1) * 128]
                mk.op("vector", "tensor_scalar", sl, sl, b("ln2w_s")[:, k:k + 1],
                      b("ln2b_s")[:, k:k + 1], op0=OP.mult, op1=OP.add,
                      inc=sv)
            mk.waitop("tensor", sv)
            for k in range(8):
                mk.op("tensor", "matmul", ps_r[:, 0:8],
                      b("hn2")[:, k * 2048 + tc * 128: k * 2048 + (tc + 1) * 128],
                      b("wbuf")[:, 8192 + k * 8: 8192 + (k + 1) * 8],
                      start=(k == 0), stop=(k == 7), inc=(sp if k == 7 else None))
            mk.op("scalar", "copy", b("glog")[:, tc * 8:(tc + 1) * 8],
                  ps_r[:, 0:8], wait=sp, inc=ss)

        emit_top2_gates(mk, b("glog"), b("gates"), b("gscr"), renorm=False,
                        gaps=True)

        if self.dbg:
            self.dump("x1i_attn", b("x1i")[:, :], (128, 16384), F32)
            self.dump("gates", b("gates")[:, :], (128, 128), F32)
            self.dump("attn_acc", b("attn_acc")[:, :], (128, 1024), F32)
            self.dump("hn2", b("hn2")[:, :], (128, 16384), BF16)
        w1h = b("l0f_w1")
        w2h = b("l0f_w2")
        for e in range(8 if self.moe else 0):
            mk.waitop("gpsimd", sp)
            mk.dma("gpsimd", b("wbuf")[:, :].rearrange("p (m c) -> p m c", m=16),
                   w1h.ap()[e, :, :, :].rearrange("m p c -> p m c"), lw)
            mk.dma("gpsimd", b("wbuf2")[:, :].rearrange("p (k n) -> p k n", k=16),
                   w2h.ap()[e, :, :].rearrange("(k p) n -> p k n", p=128), lw)
            mk.waitop("tensor", lw)
            mk.waitop("tensor", sv)
            for tg in range(8):
                t0 = tg * 256
                for fm in range(16):
                    ps = ps_mm if fm % 2 == 0 else ps_mm2
                    mk.waitop("tensor", ss)
                    for k in range(8):
                        mk.op("tensor", "matmul", ps[:, 0:256],
                              b("wbuf")[:, fm * 1024 + k * 128: fm * 1024 + (k + 1) * 128],
                              b("hn2")[:, k * 2048 + t0: k * 2048 + t0 + 256],
                              start=(k == 0), stop=(k == 7),
                              inc=(sp if k == 7 else None))
                    mk.op("scalar", "activation",
                          b("hmid")[:, fm * 256:(fm + 1) * 256], ps[:, 0:256],
                          AF.Gelu, wait=sp, inc=ss)
                for t2 in range(2):
                    tt = tg * 2 + t2
                    for n in range(2):
                        ps = ps_mm if n == 0 else ps_mm2
                        mk.waitop("tensor", sv)
                        mk.waitop("tensor", ss)
                        for k in range(16):
                            mk.op("tensor", "matmul", ps[:, :],
                                  b("hmid")[:, k * 256 + t2 * 128: k * 256 + t2 * 128 + 128],
                                  b("wbuf2")[:, k * 1024 + n * 512: k * 1024 + (n + 1) * 512],
                                  start=(k == 0), stop=(k == 15),
                                  inc=(sp if k == 15 else None))
                        mk.op("vector", "scalar_tensor_tensor",
                              b("x1i")[:, tt * 1024 + n * 512: tt * 1024 + (n + 1) * 512],
                              ps[:, :], b("gates")[:, tt * 8 + e: tt * 8 + e + 1],
                              b("x1i")[:, tt * 1024 + n * 512: tt * 1024 + (n + 1) * 512],
                              wait=sp, op0=OP.mult, op1=OP.add, inc=sv)

        for tc in range(16):
            mk.dma("sync", b("xftile")[:, :], x1_hbm[tc * 128:(tc + 1) * 128, :],
                   ld, wait=sv)
            mk.waitop("vector", ld)
            mk.op("vector", "tensor_tensor",
                  b("x1i")[:, tc * 1024:(tc + 1) * 1024],
                  b("x1i")[:, tc * 1024:(tc + 1) * 1024], b("xftile")[:, :],
                  op=OP.add, inc=sv)
            mk.dma("sync", x2_hbm[tc * 128:(tc + 1) * 128, :],
                   b("x1i")[:, tc * 1024:(tc + 1) * 1024], so, wait=sv)
        mk.waitop("sync", so)



class L1Attn:
    @staticmethod
    def prep_inputs(inp):
        out = {}
        out["l1_w_inproj"] = tile_lhsT(inp["L1_attn_h_in_proj_w"].T)
        cw = inp["L1_attn_h_conv_w"]
        out["l1_convw"] = np.ascontiguousarray(
            cw.reshape(16, 128, 4).transpose(1, 0, 2).reshape(128, 64).astype(np.float32))
        out["l1_convb"] = fm_cols(inp["L1_attn_h_conv_b"])
        out["l1_w_xproj"] = tile_lhsT(inp["L1_attn_h_x_proj_w"].T)
        out["l1_w_dtproj"] = tile_lhsT(inp["L1_attn_h_dt_proj_w"].T)
        out["l1_dtb"] = fm_cols(inp["L1_attn_h_dt_proj_b"])
        al = inp["L1_attn_h_A_log"]
        out["l1_alog"] = np.ascontiguousarray(
            al.reshape(16, 128, 16).transpose(1, 0, 2).reshape(128, 256).astype(np.float32))
        out["l1_dpar"] = fm_cols(inp["L1_attn_h_D_param"])
        out["l1_w_outproj"] = rhs_weight(inp["L1_attn_h_out_proj_w"].T)
        return out

    def __init__(self, mk: MK, dbg=False):
        self.mk = mk
        self.dbg = dbg

    def dump(self, name, ap, shape, dtype, block=True):
        mk = self.mk
        h = mk.dram_out("dbg3_" + name, shape, dtype)
        for s in ("sv", "ss", "sp", "sg"):
            mk.waitop("sync", mk.sem(s))
        mk.dma("sync", h[tuple(slice(0, n) for n in shape)], ap, mk.sem("so"))
        if block:
            for e in ("vector", "scalar", "tensor", "gpsimd"):
                mk.waitop(e, mk.sem("so"))

    def alloc(self):
        mk = self.mk
        sb = mk.sbuf
        sb("ident_bf_s", (128, 128), BF16)
        sb("ident_f_s", (128, 128), F32)
        sb("ones_f", (128, 128), F32)
        sb("c_eps", (128, 1), F32)
        sb("c_one", (128, 1), F32)
        sb("convw_s", (128, 64), F32)
        sb("convb_s", (128, 16), F32)
        sb("dtb_s", (128, 16), F32)
        sb("a1_s", (128, 256), F32)
        sb("dpar_s", (128, 16), F32)
        sb("xbc1", (128, 16 * 515), BF16)
        sb("z1f", (128, 16 * 512), BF16)
        sb("u1", (128, 16 * 512), BF16)
        sb("du1", (128, 16 * 512), BF16)
        sb("del1", (128, 16 * 512), BF16)
        sb("yac1", (128, 16 * 512), F32)
        sb("ygb1", (128, 16 * 512), BF16)
        sb("xf2q", (128, 8 * 512), BF16)
        sb("tmq", (128, 1024), BF16)
        sb("dtr_rhs", (128, 512), BF16)
        sb("bc1", (32, 512), F32)
        sb("bct", (128, 4 * 32), F32)
        sb("b_bc", (128, 512), F32)
        sb("c_bc", (128, 512), F32)
        sb("carry", (128, 256), F32)
        sb("sc_a", (128, 512), F32)
        sb("sc_b", (128, 512), F32)
        sb("sc_h", (128, 512), F32)
        sb("sc_h2", (128, 512), F32)
        sb("b_bc2", (128, 512), F32)
        sb("c_bc2", (128, 512), F32)
        sb("sc_p", (128, 512), F32)
        sb("sc_r1", (128, 512), F32)
        sb("sc_r2", (128, 512), F32)
        sb("sc_cb", (128, 128), F32)
        sb("xstage", (128, 1024), F32)
        sb("stat_nm", (128, 16), F32)
        sb("stat_rs", (128, 16), F32)
        sb("wbuf", (128, 8 * 1024), BF16)
        sb("wbuf2", (128, 16 * 512), BF16)
        mk.psum("ps_mm", (128, 512), F32)
        mk.psum("ps_mm2", (128, 512), F32)
        mk.psum("ps_tr", (128, 1024), BF16)
        mk.psum("ps_trf", (128, 512), F32)

    def emit(self, x2_hbm, x3_hbm):
        mk = self.mk
        b = mk.buf
        self.alloc()
        ld = mk.sem("ld")
        lw = mk.sem("lw")
        so = mk.sem("so")
        sv = mk.sem("sv")
        ss = mk.sem("ss")
        sp = mk.sem("sp")
        sg = mk.sem("sg")
        ps_mm, ps_mm2 = b("ps_mm"), b("ps_mm2")
        ps_tr, ps_trf = b("ps_tr"), b("ps_trf")

        for nm, dst in [("ident_bf", "ident_bf_s"), ("ident_f", "ident_f_s"),
                        ("l1_convw", "convw_s"), ("l1_convb", "convb_s"),
                        ("l1_dtb", "dtb_s"), ("l1_alog", "a1_s"),
                        ("l1_dpar", "dpar_s")]:
            mk.dma("sync", b(dst)[:, :], b(nm).ap(), ld)
        mk.op("gpsimd", "memset", b("c_eps")[:, :], LN_EPS, inc=sg)
        mk.op("gpsimd", "memset", b("c_one")[:, :], 1.0, inc=sg)
        mk.op("gpsimd", "memset", b("ones_f")[:, :], 1.0, inc=sg)
        mk.op("gpsimd", "memset", b("carry")[:, :], 0.0, inc=sg)
        mk.op("gpsimd", "memset", b("dtr_rhs")[:, :], 0.0, inc=sg)
        for ch in range(16):
            mk.op("gpsimd", "memset", b("xbc1")[:, ch * 515:ch * 515 + 3], 0.0,
                  inc=sg)
        for e in ("vector", "scalar", "tensor"):
            mk.waitop(e, ld)
            mk.waitop(e, sg)
        mk.op("scalar", "activation", b("a1_s")[:, :], b("a1_s")[:, :], AF.Exp,
              inc=ss)
        mk.op("vector", "tensor_scalar", b("a1_s")[:, :], b("a1_s")[:, :],
              -1.0, None, wait=ss, op0=OP.mult, inc=sv)

        for q in range(4):
            r0 = q * 512
            mk.waitop("vector", sp)
            for tc in range(4):
                xs = b("xstage")
                mk.dma("sync", xs[:, :], x2_hbm[r0 + tc * 128: r0 + (tc + 1) * 128, :],
                       ld, wait=sv, wait2=ss)
                mk.waitop("vector", ld)
                mk.ln_stats(xs, b("stat_nm")[:, tc:tc + 1],
                            b("stat_rs")[:, tc:tc + 1], 1024, sv, ss, wait=ld)
                mk.op("vector", "tensor_scalar", b("tmq")[:, 0:1024], xs[:, :],
                      b("stat_nm")[:, tc:tc + 1], b("stat_rs")[:, tc:tc + 1],
                      wait=ss, op0=OP.add, op1=OP.mult, inc=sv)
                for k in range(8):
                    mk.op("tensor", "transpose", ps_tr[:, 0:128],
                          b("tmq")[:, k * 128:(k + 1) * 128],
                          b("ident_bf_s")[:, :], wait=sv, wait2=ss, inc=sp)
                    mk.op("scalar", "copy",
                          b("xf2q")[:, k * 512 + tc * 128: k * 512 + (tc + 1) * 128],
                          ps_tr[:, 0:128], wait=sp, inc=ss)

            winp = b("l1_w_inproj")
            mk.waitop("scalar", sv)
            G = 8
            for g0 in range(0, 32, G):
                mk.waitop("gpsimd", sp)
                for mc in range(g0, g0 + G):
                    mk.dma("gpsimd", b("wbuf")[:, (mc - g0) * 1024:(mc - g0 + 1) * 1024],
                           winp.ap()[mc, :, :], lw)
                mk.waitop("tensor", lw)
                for mc in range(g0, g0 + G):
                    ps = ps_mm if (mc % 2 == 0) else ps_mm2
                    mk.waitop("tensor", ss)
                    for k in range(8):
                        mk.op("tensor", "matmul", ps[:, :],
                              b("wbuf")[:, (mc - g0) * 1024 + k * 128:
                                        (mc - g0) * 1024 + (k + 1) * 128],
                              b("xf2q")[:, k * 512:(k + 1) * 512],
                              start=(k == 0), stop=(k == 7),
                              inc=(sp if k == 7 else None))
                    if mc < 16:
                        mk.op("scalar", "copy",
                              b("xbc1")[:, mc * 515 + 3: mc * 515 + 515],
                              ps[:, :], wait=sp, inc=ss)
                    else:
                        mk.op("scalar", "copy",
                              b("z1f")[:, (mc - 16) * 512:(mc - 15) * 512],
                              ps[:, :], wait=sp, inc=ss)

            for ch in range(16):
                base = ch * 515
                acc = b("sc_p")
                mk.op("vector", "tensor_scalar", acc[:, :],
                      b("xbc1")[:, base:base + 512],
                      b("convw_s")[:, ch * 4: ch * 4 + 1], None, wait=ss,
                      op0=OP.mult)
                for kk in range(1, 4):
                    mk.op("vector", "scalar_tensor_tensor", acc[:, :],
                          b("xbc1")[:, base + kk: base + kk + 512],
                          b("convw_s")[:, ch * 4 + kk: ch * 4 + kk + 1], acc[:, :],
                          op0=OP.mult, op1=OP.add,
                          inc=(sv if kk == 3 else None))
                mk.op("scalar", "activation", b("u1")[:, ch * 512:(ch + 1) * 512],
                      acc[:, :], AF.Silu, wait=sv,
                      bias=b("convb_s")[:, ch:ch + 1], inc=ss)
            for ch in range(16):
                base = ch * 515
                mk.op("vector", "tensor_copy", b("xbc1")[:, base:base + 3],
                      b("xbc1")[:, base + 512:base + 515], wait=ss,
                      inc=(sv if ch == 15 else None))

            wxp = b("l1_w_xproj")
            mk.waitop("gpsimd", sp)
            mk.dma("gpsimd", b("wbuf")[:, 0:2048], wxp.ap()[0, :, :], lw)
            mk.waitop("tensor", lw)
            mk.waitop("tensor", ss)
            for k in range(16):
                mk.op("tensor", "matmul", ps_mm[:, :],
                      b("wbuf")[:, k * 128:(k + 1) * 128],
                      b("u1")[:, k * 512:(k + 1) * 512],
                      start=(k == 0), stop=(k == 15), inc=(sp if k == 15 else None))
            mk.op("scalar", "copy", b("dtr_rhs")[0:64, :], ps_mm[0:64, :],
                  wait=sp, inc=ss)
            mk.op("scalar", "copy", b("bc1")[:, :], ps_mm[64:96, :], inc=ss)

            wdt = b("l1_w_dtproj")
            mk.waitop("gpsimd", sp)
            mk.dma("gpsimd",
                   b("wbuf")[:, 2048:4096].rearrange("p (m c) -> p m c", m=16),
                   wdt.ap().rearrange("m p c -> p m c"), lw)
            mk.waitop("tensor", lw)
            for md in range(16):
                ps = ps_mm if (md % 2 == 0) else ps_mm2
                mk.waitop("tensor", ss)
                mk.op("tensor", "matmul", ps[:, :],
                      b("wbuf")[:, 2048 + md * 128: 2048 + (md + 1) * 128],
                      b("dtr_rhs")[:, :], start=True, stop=True, inc=sp)
                mk.op("scalar", "activation", b("sc_r1")[:, :], ps[:, :], AF.Relu,
                      wait=sp, bias=b("dtb_s")[:, md:md + 1])
                mk.op("scalar", "activation", b("sc_r2")[:, :], ps[:, :], AF.Abs,
                      bias=b("dtb_s")[:, md:md + 1])
                mk.op("scalar", "activation", b("sc_r2")[:, :], b("sc_r2")[:, :],
                      AF.Exp, scale=-1.0)
                mk.op("scalar", "activation", b("sc_r2")[:, :], b("sc_r2")[:, :],
                      AF.Ln, bias=b("c_one")[:, :], inc=ss)
                mk.op("vector", "tensor_tensor",
                      b("del1")[:, md * 512:(md + 1) * 512], b("sc_r1")[:, :],
                      b("sc_r2")[:, :], wait=ss, op=OP.add, inc=sv)
                mk.op("vector", "tensor_tensor",
                      b("du1")[:, md * 512:(md + 1) * 512],
                      b("del1")[:, md * 512:(md + 1) * 512],
                      b("u1")[:, md * 512:(md + 1) * 512], op=OP.mult, inc=sv)

            if self.dbg and q == 0:
                self.dump("u1", b("u1")[:, :], (128, 8192), BF16)
                self.dump("z1f", b("z1f")[:, :], (128, 8192), BF16)
                self.dump("del1", b("del1")[:, :], (128, 8192), BF16)
                self.dump("du1", b("du1")[:, :], (128, 8192), BF16)
                self.dump("dtrr", b("dtr_rhs")[:, :], (128, 512), BF16)
                self.dump("bc1d", b("bc1")[:, :], (32, 512), F32)
                self.dump("xf2q", b("xf2q")[:, :], (128, 4096), BF16)
            for tch in range(4):
                mk.op("tensor", "transpose", ps_trf[:, 0:32],
                      b("bc1")[:, tch * 128:(tch + 1) * 128],
                      b("ident_f_s")[0:32, 0:32], wait=sv, wait2=ss, inc=sp)
                mk.op("scalar", "copy", b("bct")[:, tch * 32:(tch + 1) * 32],
                      ps_trf[:, 0:32], wait=sp, inc=ss)
            def consume(k):
                nn, dd = divmod(k, 16)
                gcol2 = dd * 16 + nn
                hbuf = b("sc_h") if k % 2 == 0 else b("sc_h2")
                cbuf = b("c_bc") if nn % 2 == 0 else b("c_bc2")
                if nn == 0:
                    mk.op("vector", "tensor_tensor",
                          b("yac1")[:, dd * 512:(dd + 1) * 512],
                          hbuf[:, :], cbuf[:, :], op=OP.mult, inc=sv)
                else:
                    mk.op("vector", "tensor_tensor", b("sc_p")[:, :],
                          hbuf[:, :], cbuf[:, :], op=OP.mult)
                    mk.op("vector", "tensor_tensor",
                          b("yac1")[:, dd * 512:(dd + 1) * 512],
                          b("yac1")[:, dd * 512:(dd + 1) * 512],
                          b("sc_p")[:, :], op=OP.add, inc=sv)
                mk.op("scalar", "copy", b("carry")[:, gcol2:gcol2 + 1],
                      hbuf[:, 511:512], wait=sv, inc=ss)

            for n in range(16):
                bbuf = b("b_bc") if n % 2 == 0 else b("b_bc2")
                cbuf = b("c_bc") if n % 2 == 0 else b("c_bc2")
                for tch in range(4):
                    mk.op("vector", "tensor_scalar", b("sc_cb")[:, :],
                          b("ones_f")[:, :], b("bct")[:, tch * 32 + n: tch * 32 + n + 1],
                          None, wait=sp, wait2=ss, op0=OP.mult, inc=sv)
                    mk.op("tensor", "transpose", ps_trf[:, 128:256],
                          b("sc_cb")[:, :], b("ident_f_s")[:, :], wait=sv, inc=sp)
                    mk.op("vector", "tensor_copy",
                          bbuf[:, tch * 128:(tch + 1) * 128],
                          ps_trf[:, 128:256], wait=sp, inc=sv)
                    mk.op("vector", "tensor_scalar", b("sc_cb")[:, :],
                          b("ones_f")[:, :],
                          b("bct")[:, tch * 32 + 16 + n: tch * 32 + 17 + n],
                          None, op0=OP.mult, inc=sv)
                    mk.op("tensor", "transpose", ps_trf[:, 128:256],
                          b("sc_cb")[:, :], b("ident_f_s")[:, :], wait=sv, inc=sp)
                    mk.op("vector", "tensor_copy",
                          cbuf[:, tch * 128:(tch + 1) * 128],
                          ps_trf[:, 128:256], wait=sp, inc=sv)
                for dblk in range(16):
                    k = n * 16 + dblk
                    gcol = dblk * 16 + n
                    hbuf = b("sc_h") if k % 2 == 0 else b("sc_h2")
                    mk.op("scalar", "activation", b("sc_a")[:, :],
                          b("del1")[:, dblk * 512:(dblk + 1) * 512], AF.Exp,
                          wait=sv, scale=b("a1_s")[:, gcol:gcol + 1], inc=ss)
                    mk.op("vector", "tensor_tensor", b("sc_b")[:, :],
                          b("du1")[:, dblk * 512:(dblk + 1) * 512], bbuf[:, :],
                          wait=ss, op=OP.mult, inc=sv)
                    mk.op("vector", "tensor_tensor_scan", hbuf[:, :],
                          b("sc_a")[:, :], b("sc_b")[:, :],
                          b("carry")[:, gcol:gcol + 1], op0=OP.mult, op1=OP.add,
                          inc=sv)
                    if k > 0:
                        consume(k - 1)
                    if self.dbg and q == 0 and k in (32, 128, 255):
                        self.dump("yk%d" % k, b("yac1")[:, :], (128, 8192), F32)
            consume(255)

            if self.dbg and q == 0:
                self.dump("yac1", b("yac1")[:, :], (128, 8192), F32)
                self.dump("carry", b("carry")[:, :], (128, 256), F32)
                self.dump("sca", b("sc_a")[:, :], (128, 512), F32)
                self.dump("scb", b("sc_b")[:, :], (128, 512), F32)
                self.dump("sch2", b("sc_h2")[:, :], (128, 512), F32)
                self.dump("bbc2", b("b_bc2")[:, :], (128, 512), F32)
                self.dump("cbc2", b("c_bc2")[:, :], (128, 512), F32)
            for dblk in range(16):
                mk.op("vector", "scalar_tensor_tensor",
                      b("yac1")[:, dblk * 512:(dblk + 1) * 512],
                      b("u1")[:, dblk * 512:(dblk + 1) * 512],
                      b("dpar_s")[:, dblk:dblk + 1],
                      b("yac1")[:, dblk * 512:(dblk + 1) * 512],
                      op0=OP.mult, op1=OP.add, inc=sv)
                mk.op("scalar", "activation", b("sc_r1")[:, :],
                      b("z1f")[:, dblk * 512:(dblk + 1) * 512], AF.Silu,
                      wait=sv, inc=ss)
                mk.op("vector", "tensor_tensor",
                      b("ygb1")[:, dblk * 512:(dblk + 1) * 512],
                      b("yac1")[:, dblk * 512:(dblk + 1) * 512], b("sc_r1")[:, :],
                      wait=ss, op=OP.mult, inc=sv)

            if self.dbg and q == 0:
                self.dump("ygb1", b("ygb1")[:, :], (128, 8192), BF16)
                self.dump("yach", b("yac1")[:, :], (128, 8192), F32)
            wop = b("l1_w_outproj")
            for n in range(2):
                mk.waitop("gpsimd", sp)
                mk.dma("gpsimd",
                       b("wbuf2")[:, :].rearrange("p (k c) -> p k c", k=16),
                       wop.ap()[:, n * 512:(n + 1) * 512]
                       .rearrange("(k p) c -> p k c", p=128), lw)
                mk.waitop("tensor", lw)
                for tc in range(4):
                    ps = ps_mm if (tc % 2 == 0) else ps_mm2
                    mk.dma("sync", b("xstage")[:, n * 512:(n + 1) * 512],
                           x2_hbm[r0 + tc * 128: r0 + (tc + 1) * 128,
                                  n * 512:(n + 1) * 512], ld, wait=sv)
                    mk.waitop("tensor", ss)
                    mk.waitop("tensor", sv)
                    for k in range(16):
                        mk.op("tensor", "matmul", ps[:, :],
                              b("ygb1")[:, k * 512 + tc * 128: k * 512 + (tc + 1) * 128],
                              b("wbuf2")[:, k * 512:(k + 1) * 512],
                              start=(k == 0), stop=(k == 15),
                              inc=(sp if k == 15 else None))
                    mk.waitop("vector", ld)
                    mk.op("vector", "tensor_tensor", b("sc_p")[:, :], ps[:, :],
                          b("xstage")[:, n * 512:(n + 1) * 512], wait=sp,
                          wait2=so, op=OP.add, inc=sv)
                    mk.dma("sync",
                           x3_hbm[r0 + tc * 128: r0 + (tc + 1) * 128,
                                  n * 512:(n + 1) * 512],
                           b("sc_p")[:, :], so, wait=sv)
            mk.waitop("vector", so)
        mk.waitop("sync", so)



class L1Ffn:
    @staticmethod
    def prep_inputs(inp):
        out = {}
        out["l1f_w_router"] = rhs_weight(inp["L1_ffn_gate_weight"].T)
        sgu = inp["L1_ffn_shared_gate_up"]
        egu = inp["L1_ffn_experts_gate_up"]
        sd = inp["L1_ffn_shared_down"]
        ed = inp["L1_ffn_experts_down"]
        wg = [tile_lhsT(sgu[:938].T)] + [tile_lhsT(egu[e, :938].T) for e in range(8)]
        wu = [tile_lhsT(sgu[938:].T)] + [tile_lhsT(egu[e, 938:].T) for e in range(8)]
        wd = [rhs_weight(sd.T)] + [rhs_weight(ed[e].T) for e in range(8)]
        out["l1f_wg"] = np.stack(wg)
        out["l1f_wu"] = np.stack(wu)
        out["l1f_wd"] = np.stack(wd)
        return out

    def __init__(self, mk: MK, dbg=False):
        self.mk = mk
        self.dbg = dbg

    def alloc(self):
        mk = self.mk
        sb = mk.sbuf
        sb("ident_bf_s", (128, 128), BF16)
        sb("c_eps", (128, 1), F32)
        sb("xf3", (128, 8 * 2048), BF16)
        sb("oacc", (128, 16 * 1024), F32)
        sb("cw", (128, 16 * 8), F32)
        sb("glog", (128, 16 * 8), F32)
        sb("gscr", (128, 704), F32)
        sb("gblk", (128, 8 * 256), BF16)
        sb("ublk", (128, 8 * 256), BF16)
        sb("midb", (128, 8 * 256), BF16)
        sb("xstage", (128, 1024), F32)
        sb("tmp_tm", (128, 1024), BF16)
        sb("stat_nm", (128, 16), F32)
        sb("stat_rs", (128, 16), F32)
        sb("sc_small", (128, 32), F32)
        sb("wbuf", (128, 16384), BF16)
        sb("wbuf2", (128, 8192), BF16)
        mk.psum("ps_mm", (128, 512), F32)
        mk.psum("ps_mm2", (128, 512), F32)
        mk.psum("ps_tr", (128, 1024), BF16)
        mk.psum("ps_r", (128, 512), F32)

    def emit(self, x3_hbm, out_hbm):
        mk = self.mk
        b = mk.buf
        self.alloc()
        ld = mk.sem("ld")
        lw = mk.sem("lw")
        so = mk.sem("so")
        sv = mk.sem("sv")
        ss = mk.sem("ss")
        sp = mk.sem("sp")
        sg = mk.sem("sg")
        ps_mm, ps_mm2, ps_tr, ps_r = b("ps_mm"), b("ps_mm2"), b("ps_tr"), b("ps_r")

        mk.dma("sync", b("ident_bf_s")[:, :], b("ident_bf").ap(), ld)
        mk.op("gpsimd", "memset", b("c_eps")[:, :], LN_EPS, inc=sg)
        mk.dma("gpsimd", b("wbuf2")[:, 0:64].rearrange("p (k n) -> p k n", k=8),
               b("l1f_w_router").ap().rearrange("(k p) n -> p k n", p=128), lw)
        for e in ("vector", "scalar", "tensor"):
            mk.waitop(e, ld)
            mk.waitop(e, lw)
            mk.waitop(e, sg)

        for tc in range(16):
            xs = b("xstage")
            mk.dma("sync", xs[:, :], x3_hbm[tc * 128:(tc + 1) * 128, :], ld,
                   wait=sv, wait2=ss)
            mk.waitop("vector", ld)
            mk.op("vector", "tensor_copy", b("oacc")[:, tc * 1024:(tc + 1) * 1024],
                  xs[:, :], inc=sv)
            mk.ln_stats(xs, b("stat_nm")[:, 0:1], b("stat_rs")[:, 0:1],
                        1024, sv, ss)
            mk.op("vector", "tensor_scalar", b("tmp_tm")[:, :], xs[:, :],
                  b("stat_nm")[:, 0:1], b("stat_rs")[:, 0:1], wait=ss,
                  op0=OP.add, op1=OP.mult, inc=sv)
            for k in range(8):
                mk.op("tensor", "transpose", ps_tr[:, 0:128],
                      b("tmp_tm")[:, k * 128:(k + 1) * 128],
                      b("ident_bf_s")[:, :], wait=sv, wait2=ss, inc=sp)
                mk.op("scalar", "copy",
                      b("xf3")[:, k * 2048 + tc * 128: k * 2048 + (tc + 1) * 128],
                      ps_tr[:, 0:128], wait=sp, inc=ss)
            mk.waitop("tensor", ss)
            for k in range(8):
                mk.op("tensor", "matmul", ps_r[:, 0:8],
                      b("xf3")[:, k * 2048 + tc * 128: k * 2048 + (tc + 1) * 128],
                      b("wbuf2")[:, k * 8:(k + 1) * 8],
                      start=(k == 0), stop=(k == 7), inc=(sp if k == 7 else None))
            mk.op("scalar", "copy", b("glog")[:, tc * 8:(tc + 1) * 8],
                  ps_r[:, 0:8], wait=sp, inc=ss)

        emit_top2_gates(mk, b("glog"), b("cw"), b("gscr"), renorm=True)

        wgh, wuh, wdh = b("l1f_wg"), b("l1f_wu"), b("l1f_wd")
        for j in range(9):
            mk.waitop("gpsimd", sp)
            mk.dma("gpsimd", b("wbuf")[:, 0:8192].rearrange("p (m c) -> p m c", m=8),
                   wgh.ap()[j, :, :, :].rearrange("m p c -> p m c"), lw)
            mk.dma("gpsimd", b("wbuf")[:, 8192:16384].rearrange("p (m c) -> p m c", m=8),
                   wuh.ap()[j, :, :, :].rearrange("m p c -> p m c"), lw)
            mk.dma("gpsimd", b("wbuf2")[:, 0:8192].rearrange("p (k n) -> p k n", k=8),
                   wdh.ap()[j, :, :].rearrange("(k p) n -> p k n", p=128), lw)
            mk.waitop("tensor", lw)
            mk.waitop("tensor", sg)
            for tg in range(8):
                t0 = tg * 256
                for mc in range(8):
                    ps = ps_mm if mc % 2 == 0 else ps_mm2
                    mk.waitop("tensor", ss)
                    mk.waitop("tensor", sv)
                    for k in range(8):
                        mk.op("tensor", "matmul", ps[:, 0:256],
                              b("wbuf")[:, mc * 1024 + k * 128: mc * 1024 + (k + 1) * 128],
                              b("xf3")[:, k * 2048 + t0: k * 2048 + t0 + 256],
                              start=(k == 0), stop=(k == 7),
                              inc=(sp if k == 7 else None))
                    mk.op("scalar", "activation",
                          b("gblk")[:, mc * 256:(mc + 1) * 256], ps[:, 0:256],
                          AF.Silu, wait=sp, inc=ss)
                    ps2 = ps_mm2 if mc % 2 == 0 else ps_mm
                    for k in range(8):
                        mk.op("tensor", "matmul", ps2[:, 0:256],
                              b("wbuf")[:, 8192 + mc * 1024 + k * 128:
                                        8192 + mc * 1024 + (k + 1) * 128],
                              b("xf3")[:, k * 2048 + t0: k * 2048 + t0 + 256],
                              start=(k == 0), stop=(k == 7),
                              inc=(sp if k == 7 else None))
                    mk.op("scalar", "copy",
                          b("ublk")[:, mc * 256:(mc + 1) * 256], ps2[:, 0:256],
                          wait=sp, inc=ss)
                    mk.op("vector", "tensor_tensor",
                          b("midb")[:, mc * 256:(mc + 1) * 256],
                          b("gblk")[:, mc * 256:(mc + 1) * 256],
                          b("ublk")[:, mc * 256:(mc + 1) * 256], wait=ss,
                          op=OP.mult, inc=sv)
                for t2 in range(2):
                    tt = tg * 2 + t2
                    for n in range(2):
                        ps = ps_mm if n == 0 else ps_mm2
                        mk.waitop("tensor", sv)
                        mk.waitop("tensor", ss)
                        for k in range(8):
                            mk.op("tensor", "matmul", ps[:, :],
                                  b("midb")[:, k * 256 + t2 * 128: k * 256 + t2 * 128 + 128],
                                  b("wbuf2")[:, k * 1024 + n * 512: k * 1024 + (n + 1) * 512],
                                  start=(k == 0), stop=(k == 7),
                                  inc=(sp if k == 7 else None))
                        dst = b("oacc")[:, tt * 1024 + n * 512: tt * 1024 + (n + 1) * 512]
                        if j == 0:
                            mk.op("vector", "tensor_tensor", dst, dst, ps[:, :],
                                  wait=sp, op=OP.add, inc=sv)
                        else:
                            mk.op("vector", "scalar_tensor_tensor", dst, ps[:, :],
                                  b("cw")[:, tt * 8 + (j - 1): tt * 8 + j], dst,
                                  wait=sp, op0=OP.mult, op1=OP.add, inc=sv)

        for tc in range(16):
            mk.dma("sync", out_hbm[tc * 128:(tc + 1) * 128, :],
                   b("oacc")[:, tc * 1024:(tc + 1) * 1024], so, wait=sv)
        mk.waitop("sync", so)



_CACHE = {}


def _build_l0attn():
    nc = bass.Bass("TRN2", target_bir_lowering=False)
    mkk = MK(nc)
    x_h = mkk.dram_in("x", np.zeros((2048, 1024), np.float32))
    x1_h = mkk.dram_out("x1", (2048, 1024), F32)
    q_h = mkk.dram_out("qout", (2048, 1024), BF16)
    kv_h = mkk.dram_out("kvout", (2048, 2048), BF16)
    for k in ("l0_w_inproj", "l0_convw", "l0_convb", "l0_dtbias", "l0_alog",
              "l0_d128", "l0_normw", "l0_w_outproj", "l0_w_toh", "l0f_ln1w",
              "l0f_ln1b", "l0f_w_qkv", "l0f_qkvb", "cmask", "ident_bf",
              "ident_f"):
        mkk.dram_in(k, _CACHE["prep"][k])
    with mkk.ctx:
        with nc.Block() as block:
            L0Attn(mkk).emit(x_h, x1_h, q_h, kv_h)
            mkk.finalize_and_emit(block)
    return nc, dict(mkk.inputs)


def _build_l0ffn():
    nc = bass.Bass("TRN2", target_bir_lowering=False)
    mkk = MK(nc)
    x1_h = mkk.dram_in("x1in", np.zeros((2048, 1024), np.float32))
    q_h = mkk.dram_in("qin", np.zeros((2048, 1024), ml_dtypes.bfloat16))
    kv_h = mkk.dram_in("kvall", np.zeros((8, 2048, 2048), ml_dtypes.bfloat16))
    rm_h = mkk.dram_in("rmask", np.zeros((1, 8), np.float32))
    x2_h = mkk.dram_out("x2", (2048, 1024), F32)
    for k in ("l0f_w_attnout", "l0f_attnob", "l0f_ln2w", "l0f_ln2b",
              "l0f_w_router", "l0f_w1", "l0f_w2", "ident_bf"):
        mkk.dram_in(k, _CACHE["prep"][k])
    with mkk.ctx:
        with nc.Block() as block:
            L0Ffn(mkk).emit(x1_h, q_h, kv_h, x2_h)
            mkk.finalize_and_emit(block)
    return nc, dict(mkk.inputs)


def _build_l1attn():
    nc = bass.Bass("TRN2", target_bir_lowering=False)
    mkk = MK(nc)
    x2_h = mkk.dram_in("x2in", np.zeros((2048, 1024), np.float32))
    x3_h = mkk.dram_out("x3", (2048, 1024), F32)
    for k in ("l1_w_inproj", "l1_convw", "l1_convb", "l1_w_xproj",
              "l1_w_dtproj", "l1_dtb", "l1_alog", "l1_dpar", "l1_w_outproj",
              "ident_bf", "ident_f"):
        mkk.dram_in(k, _CACHE["prep"][k])
    with mkk.ctx:
        with nc.Block() as block:
            L1Attn(mkk).emit(x2_h, x3_h)
            mkk.finalize_and_emit(block)
    return nc, dict(mkk.inputs)


def _build_l1ffn():
    nc = bass.Bass("TRN2", target_bir_lowering=False)
    mkk = MK(nc)
    x3_h = mkk.dram_in("x3in", np.zeros((2048, 1024), np.float32))
    o_h = mkk.dram_out("out", (2048, 1024), F32)
    for k in ("l1f_w_router", "l1f_wg", "l1f_wu", "l1f_wd", "ident_bf"):
        mkk.dram_in(k, _CACHE["prep"][k])
    with mkk.ctx:
        with nc.Block() as block:
            L1Ffn(mkk).emit(x3_h, o_h)
            mkk.finalize_and_emit(block)
    return nc, dict(mkk.inputs)


def _prep_all(inputs):
    prep = {}
    prep.update(L0Attn.prep_inputs(inputs))
    prep.update(L0Ffn.prep_inputs(inputs))
    prep.update(L1Attn.prep_inputs(inputs))
    prep.update(L1Ffn.prep_inputs(inputs))
    prep["ident_bf"] = np.eye(128, dtype=ml_dtypes.bfloat16)
    prep["ident_f"] = np.eye(128, dtype=np.float32)
    return prep


def kernel(**inputs):
    _CACHE["prep"] = _prep_all(inputs)
    if "p1" not in _CACHE:
        _CACHE["p1"] = _build_l0attn()
        _CACHE["p2"] = _build_l0ffn()
        _CACHE["p3"] = _build_l1attn()
        _CACHE["p4"] = _build_l1ffn()
    else:
        for key in ("p1", "p2", "p3", "p4"):
            nc, im = _CACHE[key]
            for k in list(im.keys()):
                if k in _CACHE["prep"]:
                    im[k] = _CACHE["prep"][k]

    X = np.asarray(inputs["L0_attn_X"], np.float32)

    def run2(nc, maps):
        run_bass_kernel_spmd(nc, maps, core_ids=list(range(8)))
        return run_bass_kernel_spmd(nc, maps, core_ids=list(range(8))).results

    nc1, im1 = _CACHE["p1"]
    maps = []
    for c in range(8):
        m = dict(im1)
        m["x"] = np.ascontiguousarray(X[c])
        maps.append(m)
    r1 = run2(nc1, maps)

    kv_all = np.ascontiguousarray(np.stack([r["kvout"] for r in r1]))

    nc2, im2 = _CACHE["p2"]
    maps = []
    for c in range(8):
        m = dict(im2)
        m["x1in"] = r1[c]["x1"]
        m["qin"] = r1[c]["qout"]
        m["kvall"] = kv_all
        rm = np.full((1, 8), -30000.0, np.float32)
        rm[0, :c + 1] = 0.0
        m["rmask"] = rm
        maps.append(m)
    r2 = run2(nc2, maps)

    nc3, im3 = _CACHE["p3"]
    maps = []
    for c in range(8):
        m = dict(im3)
        m["x2in"] = r2[c]["x2"]
        maps.append(m)
    r3 = run2(nc3, maps)

    nc4, im4 = _CACHE["p4"]
    maps = []
    for c in range(8):
        m = dict(im4)
        m["x3in"] = r3[c]["x3"]
        maps.append(m)
    r4 = run2(nc4, maps)

    return np.ascontiguousarray(np.stack([r["out"] for r in r4]))


# version 06
# speedup vs baseline: 1.8039x; 1.8039x over previous
import math
from contextlib import ExitStack

import numpy as np
import ml_dtypes
import concourse.bass as bass
import concourse.mybir as mybir
from concourse.alu_op_type import AluOpType as OP
from concourse.bass_utils import run_bass_kernel_spmd

F32 = mybir.dt.float32
BF16 = mybir.dt.bfloat16
AF = mybir.ActivationFunctionType
AX = mybir.AxisListType

MASK_BIG = 1e9
LN_EPS = 1e-5

ENGINES = ("sync", "gpsimd", "vector", "scalar", "tensor")

SELF_SYNC_METHS = {"tensor_reduce", "reduce_max", "reduce_sum", "bn_stats",
                   "bn_aggr", "tensor_tensor_scan", "max_index", "reciprocal",
                   "stream_shuffle", "activation", "copy"}



def tile_lhsT(wT: np.ndarray) -> np.ndarray:
    K, M = wT.shape
    nk, nm = -(-K // 128), -(-M // 128)
    H = np.zeros((nk * 128, nm * 128), np.float32)
    H[:K, :M] = wT
    H = H.reshape(nk, 128, nm, 128).transpose(2, 1, 0, 3).reshape(nm, 128, nk * 128)
    return np.ascontiguousarray(H)


def rhs_weight(w: np.ndarray) -> np.ndarray:
    K, N = w.shape
    nk = -(-K // 128)
    H = np.zeros((nk * 128, N), np.float32)
    H[:K, :] = w
    return np.ascontiguousarray(H)


def fm_cols(vec: np.ndarray) -> np.ndarray:
    C = vec.shape[0]
    assert C % 128 == 0
    return np.ascontiguousarray(vec.reshape(C // 128, 128).T.astype(np.float32))


def causal_maskbig(Q=128) -> np.ndarray:
    m = np.zeros((Q, Q), np.float32)
    m[np.tril_indices(Q, -1)] = MASK_BIG
    return m



class Sem:
    __slots__ = ("h", "n", "name")

    def __init__(self, h, name):
        self.h, self.n, self.name = h, 0, name


class MK:

    def __init__(self, nc):
        self.nc = nc
        self.ctx = ExitStack()
        self._sems = {}
        self._bufs = {}
        self.streams = {e: [] for e in ENGINES}
        self.inputs = {}

    def sem(self, name) -> Sem:
        if name not in self._sems:
            h = self.ctx.enter_context(self.nc.semaphore(name))
            self._sems[name] = Sem(h, name)
        return self._sems[name]

    def sbuf(self, name, shape, dtype):
        if name not in self._bufs:
            self._bufs[name] = self.ctx.enter_context(
                self.nc.sbuf_tensor(name, list(shape), dtype))
        return self._bufs[name]

    def psum(self, name, shape, dtype=F32):
        if name not in self._bufs:
            self._bufs[name] = self.ctx.enter_context(
                self.nc.psum_tensor(name, list(shape), dtype))
        return self._bufs[name]

    def dram_in(self, name, arr):
        if name not in self._bufs:
            a = np.ascontiguousarray(arr)
            dt = {np.dtype(np.float32): F32,
                  np.dtype(ml_dtypes.bfloat16): BF16}[a.dtype]
            self._bufs[name] = self.nc.dram_tensor(name, list(a.shape), dt,
                                                   kind="ExternalInput")
            self.inputs[name] = a
        return self._bufs[name]

    def dram_out(self, name, shape, dtype):
        if name not in self._bufs:
            self._bufs[name] = self.nc.dram_tensor(name, list(shape), dtype,
                                                   kind="ExternalOutput")
        return self._bufs[name]

    def dram_tmp(self, name, shape, dtype):
        if name not in self._bufs:
            self._bufs[name] = self.nc.dram_tensor(name, list(shape), dtype)
        return self._bufs[name]

    def buf(self, name):
        return self._bufs[name]

    def op(self, eng, meth, *args, wait=None, wait2=None, inc=None, incn=1, **kw):
        selfsync = meth not in ("dma_start", "matmul", "transpose")
        if selfsync and inc is None:
            inc = self.sem("selfs_" + eng)
        waits = []
        for w in (wait, wait2):
            if w is not None:
                waits.append((w.h, w.n))
        it = None
        if inc is not None:
            inc.n += incn
            it = (inc.h, incn)

        def f(e, meth=meth, args=args, kw=kw, waits=waits, it=it):
            for (h, n) in waits:
                e.wait_ge(h, n)
            inst = getattr(e, meth)(*args, **kw)
            if it is not None:
                inst.then_inc(it[0], it[1])
        self.streams[eng].append(f)
        if selfsync:
            self.waitop(eng, inc)

    def raw(self, eng, fn):
        self.streams[eng].append(fn)

    def waitop(self, eng, sem, n=None):
        tgt = sem.n if n is None else n

        def f(e, h=sem.h, tgt=tgt):
            e.wait_ge(h, tgt)
        self.streams[eng].append(f)

    def gap(self, eng, n=3):
        g = self.sbuf("gap_scr", (128, 512), F32)
        for _ in range(n):
            if eng == "vector":
                self.op("vector", "tensor_copy", g[:, :], g[:, :])
            elif eng == "scalar":
                self.op("scalar", "copy", g[:, :], g[:, :])
            elif eng == "gpsimd":
                self.op("gpsimd", "memset", g[:, :], 0.0)

    def gwait(self, eng, sem, n=3):
        self.waitop(eng, sem)
        self.gap(eng, n)

    def poison_all(self, value=-1e30):
        sg = self.sem("sg")
        for name, h in list(self._bufs.items()):
            try:
                shape = h.shape
                dt = h.dtype
            except Exception:
                continue
            if str(h.__class__.__name__) not in ("SBTensorHandle",):
                continue
            if dt not in (F32, BF16):
                continue
            self.op("gpsimd", "memset", h[tuple(slice(0, n) for n in shape)],
                    value, inc=sg)
        for e in ("vector", "scalar", "tensor", "sync"):
            self.waitop(e, sg)

    def ln_stats(self, src_ap, nm_dst, rs_dst, width, sv, ss, wait=None,
                 gaps=False):
        b = self.buf
        self.sbuf("bn_sc", (128, 16), F32)
        nchunk = width // 512
        for i in range(nchunk):
            self.op("vector", "bn_stats", b("bn_sc")[:, i * 6:(i + 1) * 6],
                    src_ap[:, i * 512:(i + 1) * 512],
                    wait=(wait if i == 0 else None))
        self.op("vector", "bn_aggr", b("bn_sc")[:, 12:14],
                b("bn_sc")[:, 0:6 * nchunk])
        self.op("vector", "tensor_scalar", nm_dst, b("bn_sc")[:, 12:13],
                -1.0, None, op0=OP.mult, inc=sv)
        if gaps:
            self.gwait("scalar", sv)
            self.op("scalar", "activation", b("bn_sc")[:, 14:15],
                    b("bn_sc")[:, 13:14], AF.Ln, bias=b("c_eps")[:, :])
        else:
            self.op("scalar", "activation", b("bn_sc")[:, 14:15],
                    b("bn_sc")[:, 13:14], AF.Ln, wait=sv, bias=b("c_eps")[:, :])
        self.op("scalar", "activation", rs_dst, b("bn_sc")[:, 14:15],
                AF.Exp, scale=-0.5, inc=ss)

    def dma(self, q, out, in_, sem, **kw):
        self.op(q, "dma_start", wait=kw.pop("wait", None), wait2=kw.pop("wait2", None),
                inc=sem, incn=16, out=out, in_=in_, **kw)

    def finalize_and_emit(self, block):
        for e in ENGINES:
            ops = self.streams[e]
            if not ops:
                continue

            def body(eng, ops=ops):
                for f in ops:
                    f(eng)
            getattr(block, e)(body)



class L0Attn:

    def __init__(self, mk: MK, tag="", dbg=False):
        self.mk = mk
        self.tag = tag
        self.dbg = dbg

    def dump(self, name, ap, shape, dtype):
        mk = self.mk
        h = mk.dram_out("dbg_" + name, shape, dtype)
        for s in ("sv", "ss", "sp", "sg"):
            mk.waitop("sync", mk.sem(s))
        mk.dma("sync", h[tuple(slice(0, n) for n in shape)], ap, mk.sem("so"))

    @staticmethod
    def prep_inputs(inp: dict) -> dict:
        out = {}
        out["l0_w_inproj"] = tile_lhsT(inp["L0_attn_in_proj_w"].T)
        cw = inp["L0_attn_conv_w"].reshape(2176, 4)
        cwf = np.zeros((128, 17 * 4), np.float32)
        for c in range(17):
            cwf[:, c * 4:(c + 1) * 4] = cw[c * 128:(c + 1) * 128, :]
        out["l0_convw"] = cwf
        out["l0_convb"] = fm_cols(inp["L0_attn_conv_b"])
        out["l0_dtbias"] = inp["L0_attn_dt_bias"].reshape(32, 1).astype(np.float32)
        out["l0_alog"] = inp["L0_attn_A_log"].reshape(32, 1).astype(np.float32)
        out["l0_d128"] = fm_cols(np.repeat(inp["L0_attn_D"], 64))
        out["l0_normw"] = fm_cols(inp["L0_attn_norm_w"])
        out["l0_w_outproj"] = tile_lhsT(inp["L0_attn_out_proj_w"].T)
        out["l0_w_toh"] = rhs_weight(inp["L0_attn_to_h_w"][:1024].T)
        out["l0f_ln1w"] = fm_cols(inp["L0_ffn_ln1_w"])
        out["l0f_ln1b"] = fm_cols(inp["L0_ffn_ln1_b"])
        out["l0f_w_qkv"] = rhs_weight(inp["L0_ffn_in_proj_w"].T)
        out["l0f_qkvb"] = inp["L0_ffn_in_proj_b"].reshape(1, 3072).astype(np.float32)
        out["cmask"] = causal_maskbig()
        out["ident_bf"] = np.eye(128, dtype=ml_dtypes.bfloat16)
        out["ident_f"] = np.eye(128, dtype=np.float32)
        return out

    def alloc(self):
        mk = self.mk
        sb = mk.sbuf
        sb("ident_bf_s", (128, 128), BF16)
        sb("ident_f_s", (128, 128), F32)
        sb("cmask_s", (128, 128), F32)
        sb("ones_f", (128, 128), F32)
        sb("ones_bf", (128, 128), BF16)
        sb("convw_s", (128, 68), F32)
        sb("convb_s", (128, 17), F32)
        sb("dtbias_s", (32, 1), F32)
        sb("acol_s", (32, 1), F32)
        sb("abc_s", (128, 32), F32)
        sb("d128_s", (128, 16), F32)
        sb("normw_s", (128, 16), F32)
        sb("ln1w_s", (128, 8), F32)
        sb("ln1b_s", (128, 8), F32)
        sb("qkvb_s", (1, 3072), BF16)
        sb("hrun", (64, 32 * 64), F32)
        sb("xbc", (128, 17 * 515), BF16)
        sb("zf", (128, 16 * 512), BF16)
        sb("xf", (128, 16 * 512), BF16)
        sb("bcu", (128, 512), BF16)
        sb("c0", (64, 512), BF16)
        sb("xtm", (128, 4 * 2048), BF16)
        sb("btm", (128, 4 * 64), BF16)
        sb("dtq", (32, 512), F32)
        sb("dcum", (32, 512), F32)
        sb("w2r", (64, 512), F32)
        sb("dtT", (128, 4 * 32), F32)
        sb("DtT", (128, 4 * 32), F32)
        sb("stm", (128, 4 * 32), F32)
        sb("dbc", (128, 4 * 32), F32)
        sb("yasm", (128, 16 * 128), F32)
        sb("ygf", (128, 16 * 512), BF16)
        sb("x1q", (128, 4 * 1024), F32)
        sb("tmq", (128, 4 * 1024), BF16)
        sb("fmq", (128, 8 * 512), BF16)
        sb("hf", (128, 8 * 512), BF16)
        sb("wbuf", (128, 8192), BF16)
        sb("wbuf2", (128, 8 * 1024), BF16)
        sb("sc_big", (128, 1024), F32)
        sb("sc_a", (128, 128), F32)
        sb("sc_l", (128, 128), F32)
        sb("sc_g", (128, 128), F32)
        sb("sc_m", (128, 128), BF16)
        sb("sc_cw", (64, 128), BF16)
        sb("sc_cb", (128, 128), F32)
        sb("sc_cb2", (128, 64), F32)
        sb("w2t", (128, 128), F32)
        sb("sc_sh", (32, 512), F32)
        sb("sc_bsc", (128, 64), BF16)
        sb("sc_sz", (128, 128), F32)
        sb("sc_small", (64, 64), F32)
        sb("ssrow", (1, 512), F32)
        sb("ssacc", (1, 128), F32)
        sb("stat_nm", (128, 16), F32)
        sb("stat_rs", (128, 16), F32)
        sb("stat_t1", (128, 1), F32)
        sb("stat_t2", (128, 1), F32)
        mk.psum("ps_mm", (128, 512), F32)
        mk.psum("ps_mm2", (128, 512), F32)
        mk.psum("ps_tr", (128, 1024), BF16)
        mk.psum("ps_trf", (128, 512), F32)
        mk.psum("ps_row", (128, 512), F32)
        mk.psum("ps_y", (128, 512), F32)
        mk.psum("ps_g", (128, 512), F32)
        mk.psum("ps_h", (128, 512), F32)

    def load_consts(self):
        mk = self.mk
        b = mk.buf
        ld = mk.sem("c_ld")
        sv = mk.sem("sv")
        ss = mk.sem("ss")
        sp = mk.sem("sp")
        for nm, dst in [("ident_bf", "ident_bf_s"), ("ident_f", "ident_f_s"),
                        ("cmask", "cmask_s"), ("l0_convw", "convw_s"),
                        ("l0_convb", "convb_s"), ("l0_dtbias", "dtbias_s"),
                        ("l0_d128", "d128_s"), ("l0_normw", "normw_s"),
                        ("l0f_ln1w", "ln1w_s"), ("l0f_ln1b", "ln1b_s")]:
            mk.dma("sync", b(dst)[:, :], b(nm).ap(), ld)
        mk.dma("gpsimd", b("qkvb_s")[:, :], b("l0f_qkvb").ap(), mk.sem("lw"))
        mk.dma("sync", b("acol_s")[:, :], b("l0_alog").ap(), ld)
        mk.sbuf("c_eps", (128, 1), F32)
        mk.sbuf("c_one", (128, 1), F32)
        mk.op("gpsimd", "memset", b("c_eps")[:, :], LN_EPS, inc=mk.sem("sg"))
        mk.op("gpsimd", "memset", b("c_one")[:, :], 1.0, inc=mk.sem("sg"))
        mk.op("gpsimd", "memset", b("ones_f")[:, :], 1.0, inc=mk.sem("sg"))
        mk.op("gpsimd", "memset", b("ones_bf")[:, :], 1.0, inc=mk.sem("sg"))
        for e in ("vector", "scalar", "tensor"):
            mk.waitop(e, ld)
            mk.waitop(e, mk.sem("sg"))
        mk.op("scalar", "activation", b("acol_s")[:, :], b("acol_s")[:, :], AF.Exp,
              inc=ss)
        mk.op("vector", "tensor_scalar", b("acol_s")[:, :], b("acol_s")[:, :],
              -1.0, None, wait=ss, op0=OP.mult, inc=sv)
        mk.op("tensor", "transpose", mk.buf("ps_trf")[0:1, 0:32],
              b("acol_s")[:, :], b("ident_f_s")[0:32, 0:32], wait=sv, inc=sp)
        mk.op("vector", "tensor_copy", b("sc_small")[0:1, 0:32],
              mk.buf("ps_trf")[0:1, 0:32], wait=sp, inc=sv)
        mk.op("tensor", "matmul", mk.buf("ps_row")[:, 0:32],
              b("ones_f")[0:1, :], b("sc_small")[0:1, 0:32], wait=sv,
              start=True, stop=True, inc=sp)
        mk.op("vector", "tensor_copy", b("abc_s")[:, :],
              mk.buf("ps_row")[:, 0:32], wait=sp, inc=sv)
        mk.op("gpsimd", "memset", b("hrun")[:, :], 0.0, inc=mk.sem("sg"))
        mk.op("gpsimd", "memset", b("hrun_bf")[:, :], 0.0, inc=mk.sem("sg"))
        xbc = b("xbc")
        for ch in range(17):
            mk.op("gpsimd", "memset", xbc[:, ch * 515: ch * 515 + 3], 0.0,
                  inc=mk.sem("sg"))
        for e in ("vector", "scalar", "tensor"):
            mk.waitop(e, mk.sem("c_ld"))
            mk.waitop(e, mk.sem("lw"))
            mk.waitop(e, mk.sem("sg"))

    def quarter(self, q, x_hbm, x1_hbm, q_hbm, kv_hbm):
        mk = self.mk
        b = mk.buf
        ld = mk.sem("ld")
        lw = mk.sem("lw")
        so = mk.sem("so")
        sv = mk.sem("sv")
        ss = mk.sem("ss")
        sp = mk.sem("sp")
        sg = mk.sem("sg")
        TQ = 512
        r0 = q * TQ

        ps_mm, ps_mm2 = b("ps_mm"), b("ps_mm2")
        ps_tr, ps_trf = b("ps_tr"), b("ps_trf")
        ps_row, ps_y, ps_g, ps_h = b("ps_row"), b("ps_y"), b("ps_g"), b("ps_h")

        mk.waitop("vector", sp)
        for tc in range(4):
            xs = b("sc_big")
            mk.dma("sync", xs[:, :],
                   x_hbm[r0 + tc * 128: r0 + (tc + 1) * 128, :], ld,
                   wait=sv, wait2=ss)
            mk.ln_stats(xs, b("stat_nm")[:, tc:tc + 1], b("stat_rs")[:, tc:tc + 1],
                        1024, sv, ss, wait=ld)
            mk.op("vector", "tensor_scalar", b("tmq")[:, tc * 1024:(tc + 1) * 1024],
                  xs[:, :], b("stat_nm")[:, tc:tc + 1], b("stat_rs")[:, tc:tc + 1],
                  wait=ss, op0=OP.add, op1=OP.mult, inc=sv)
        for tc in range(4):
            for k in range(8):
                mk.op("tensor", "transpose", ps_tr[:, 0:128],
                      b("tmq")[:, tc * 1024 + k * 128: tc * 1024 + (k + 1) * 128],
                      b("ident_bf_s")[:, :], wait=sv, wait2=ss, inc=sp)
                mk.op("scalar", "copy",
                      b("fmq")[:, k * 512 + tc * 128: k * 512 + (tc + 1) * 128],
                      ps_tr[:, 0:128], wait=sp, inc=ss)

        if self.dbg and q == 0:
            self.dump("fmq_lnx", b("fmq")[:, :], (128, 4096), BF16)
            self.dump("stat_nm", b("stat_nm")[:, :], (128, 16), F32)
            self.dump("stat_rs", b("stat_rs")[:, :], (128, 16), F32)
            self.dump("tmq_lnx", b("tmq")[:, :], (128, 4096), BF16)
        winp = b("l0_w_inproj")
        G = 8
        mk.waitop("scalar", sv)
        for g0 in range(0, 34, G):
            g1 = min(g0 + G, 34)
            mk.waitop("gpsimd", sp)
            for mc in range(g0, g1):
                mk.dma("gpsimd", b("wbuf")[:, (mc - g0) * 1024:(mc - g0 + 1) * 1024],
                       winp.ap()[mc, :, :], lw)
            mk.waitop("tensor", lw)
            for mc in range(g0, g1):
                ps = ps_mm if (mc % 2 == 0) else ps_mm2
                mk.waitop("tensor", ss)
                for k in range(8):
                    mk.op("tensor", "matmul", ps[:, :],
                          b("wbuf")[:, (mc - g0) * 1024 + k * 128:
                                    (mc - g0) * 1024 + (k + 1) * 128],
                          b("fmq")[:, k * 512:(k + 1) * 512],
                          start=(k == 0), stop=(k == 7),
                          inc=(sp if k == 7 else None))
                if mc < 16:
                    mk.op("scalar", "copy", b("zf")[:, mc * 512:(mc + 1) * 512],
                          ps[:, :], wait=sp, inc=ss)
                elif mc < 33:
                    c = mc - 16
                    mk.op("scalar", "copy",
                          b("xbc")[:, c * 515 + 3: c * 515 + 515],
                          ps[:, :], wait=sp, inc=ss)
                else:
                    mk.op("scalar", "copy", b("sc_big")[0:32, 512:1024], ps[0:32, :],
                          wait=sp, inc=ss)
        t32 = b("sc_sh")
        mk.op("scalar", "activation", b("dtq")[:, :], b("sc_big")[0:32, 512:1024], AF.Relu,
              wait=ss, bias=b("dtbias_s")[:, :])
        mk.op("scalar", "activation", t32[0:32, :], b("sc_big")[0:32, 512:1024], AF.Abs,
              bias=b("dtbias_s")[:, :])
        mk.op("scalar", "activation", t32[0:32, :], t32[0:32, :], AF.Exp, scale=-1.0)
        mk.op("scalar", "activation", t32[0:32, :], t32[0:32, :], AF.Ln, bias=b("c_one")[0:32, :],
              inc=ss)
        mk.op("vector", "tensor_tensor", b("dtq")[:, :], b("dtq")[:, :],
              t32[0:32, :], wait=ss, op=OP.add, inc=sv)

        if self.dbg and q == 0:
            self.dump("zf", b("zf")[:, :], (128, 8192), BF16)
            self.dump("xbc", b("xbc")[:, :], (128, 8755), BF16)
            self.dump("dtq", b("dtq")[:, :], (32, 512), F32)
        for ch in range(17):
            base = ch * 515
            acc = b("sc_big")[:, 0:512]
            mk.op("vector", "tensor_scalar", acc,
                  b("xbc")[:, base:base + 512],
                  b("convw_s")[:, ch * 4: ch * 4 + 1], None, wait=ss, op0=OP.mult)
            for kk in range(1, 4):
                mk.op("vector", "scalar_tensor_tensor", acc,
                      b("xbc")[:, base + kk: base + kk + 512],
                      b("convw_s")[:, ch * 4 + kk: ch * 4 + kk + 1], acc,
                      op0=OP.mult, op1=OP.add,
                      inc=(sv if kk == 3 else None))
            dest = (b("xf")[:, ch * 512:(ch + 1) * 512] if ch < 16
                    else b("bcu")[:, :])
            mk.op("scalar", "activation", dest, acc, AF.Silu, wait=sv,
                  bias=b("convb_s")[:, ch:ch + 1], inc=ss)
        mk.op("vector", "tensor_copy", b("c0")[:, :], b("bcu")[64:128, :],
              wait=ss, inc=sv)
        for ch in range(17):
            base = ch * 515
            mk.op("vector", "tensor_copy", b("xbc")[:, base:base + 3],
                  b("xbc")[:, base + 512:base + 515],
                  inc=(sv if ch == 16 else None))

        if self.dbg and q == 0:
            self.dump("xf", b("xf")[:, :], (128, 8192), BF16)
            self.dump("bcu", b("bcu")[:, :], (128, 512), BF16)
        for cl in range(4):
            s0 = cl * 128
            mk.op("vector", "tensor_tensor_scan", b("dcum")[:, s0:s0 + 128],
                  b("ones_f")[0:32, 0:128], b("dtq")[:, s0:s0 + 128], 0.0,
                  op0=OP.mult, op1=OP.add, inc=sv)
            mk.op("scalar", "activation", b("w2r")[0:32, s0:s0 + 128],
                  b("dcum")[:, s0:s0 + 128], AF.Exp, wait=sv,
                  scale=b("acol_s")[:, :], inc=ss)
            mk.op("tensor", "transpose", ps_trf[:, 0:32], b("dtq")[:, s0:s0 + 128],
                  b("ident_f_s")[0:32, 0:32], wait=sv, inc=sp)
            mk.op("vector", "tensor_copy", b("dtT")[:, cl * 32:(cl + 1) * 32],
                  ps_trf[:, 0:32], wait=sp, inc=sv)
            mk.op("tensor", "transpose", ps_trf[:, 32:64], b("dcum")[:, s0:s0 + 128],
                  b("ident_f_s")[0:32, 0:32], wait=sv, inc=sp)
            mk.op("vector", "tensor_copy", b("DtT")[:, cl * 32:(cl + 1) * 32],
                  ps_trf[:, 32:64], wait=sp, inc=sv)
            sf = b("sc_sh")
            mk.op("vector", "tensor_scalar", sf[0:32, 0:128],
                  b("dcum")[:, s0:s0 + 128],
                  b("dcum")[:, s0 + 127:s0 + 128], -1.0,
                  op0=OP.subtract, op1=OP.mult, inc=sv)
            mk.op("scalar", "activation", sf[0:32, 0:128], sf[0:32, 0:128], AF.Exp,
                  wait=sv, scale=b("acol_s")[:, :], inc=ss)
            mk.op("vector", "tensor_tensor", sf[0:32, 0:128], sf[0:32, 0:128],
                  b("dtq")[:, s0:s0 + 128], wait=ss, op=OP.mult, inc=sv)
            mk.op("tensor", "transpose", ps_trf[:, 64:96], sf[0:32, 0:128],
                  b("ident_f_s")[0:32, 0:32], wait=sv, inc=sp)
            mk.op("vector", "tensor_copy", b("stm")[:, cl * 32:(cl + 1) * 32],
                  ps_trf[:, 64:96], wait=sp, inc=sv)
            mk.op("scalar", "activation", b("sc_small")[0:32, 0:1],
                  b("dcum")[:, s0 + 127:s0 + 128], AF.Exp,
                  scale=b("acol_s")[:, :], inc=ss)
            mk.op("tensor", "transpose", ps_trf[0:1, 96:128],
                  b("sc_small")[0:32, 0:1], b("ident_f_s")[0:32, 0:32], wait=ss, inc=sp)
            mk.op("vector", "tensor_copy", b("sc_small")[0:1, 32:64],
                  ps_trf[0:1, 96:128], wait=sp, inc=sv)
            mk.op("tensor", "matmul", ps_row[:, 0:32], b("ones_f")[0:1, :],
                  b("sc_small")[0:1, 32:64], wait=sv, start=True, stop=True, inc=sp)
            mk.op("vector", "tensor_copy", b("dbc")[:, cl * 32:(cl + 1) * 32],
                  ps_row[:, 0:32], wait=sp, inc=sv)
            for dk in range(16):
                mk.op("tensor", "transpose", ps_tr[:, 128:256],
                      b("xf")[:, dk * 512 + s0: dk * 512 + s0 + 128],
                      b("ident_bf_s")[:, :], wait=sv, wait2=ss, inc=sp)
                mk.op("scalar", "copy",
                      b("xtm")[:, cl * 2048 + dk * 128: cl * 2048 + (dk + 1) * 128],
                      ps_tr[:, 128:256], wait=sp, inc=ss)
            mk.op("tensor", "transpose", ps_tr[:, 256:320],
                  b("bcu")[0:64, s0:s0 + 128], b("ident_bf_s")[0:64, 0:64],
                  wait=sv, wait2=ss, inc=sp)
            mk.op("scalar", "copy", b("btm")[:, cl * 64:(cl + 1) * 64],
                  ps_tr[:, 256:320], wait=sp, inc=ss)
            mk.op("tensor", "transpose", ps_trf[:, 0:32],
                  b("w2r")[0:32, s0:s0 + 128], b("ident_f_s")[0:32, 0:32],
                  wait=sv, wait2=ss, inc=sp)
            mk.op("vector", "tensor_copy", b("w2t")[:, cl * 32:(cl + 1) * 32],
                  ps_trf[:, 0:32], wait=sp, inc=sv)

        for cl in range(4):
            s0 = cl * 128
            cc = cl * 32
            mk.waitop("tensor", sv)
            mk.op("tensor", "matmul", ps_g[:, 0:128], b("bcu")[0:64, s0:s0 + 128],
                  b("c0")[0:64, s0:s0 + 128], start=True, stop=True, inc=sp)
            mk.op("scalar", "copy", b("sc_g")[:, :], ps_g[:, 0:128], wait=sp,
                  wait2=sv, inc=ss)
            for h in range(32):
                col = cc + h
                mk.op("vector", "tensor_scalar", b("sc_cb")[:, :],
                      b("ones_f")[:, :], b("DtT")[:, col:col + 1], None,
                      wait=sp, op0=OP.mult, inc=sv)
                mk.op("tensor", "transpose", ps_row[:, 128:256], b("sc_cb")[:, :],
                      b("ident_f_s")[:, :], wait=sv, inc=sp)
                mk.op("vector", "scalar_tensor_tensor", b("sc_a")[:, :],
                      ps_row[:, 128:256], b("DtT")[:, col:col + 1],
                      b("cmask_s")[:, :], wait=sp, op0=OP.subtract, op1=OP.add,
                      inc=sv)
                mk.op("scalar", "activation", b("sc_l")[:, :], b("sc_a")[:, :],
                      AF.Exp, wait=sv, scale=b("abc_s")[:, h:h + 1], inc=ss)
                mk.op("vector", "scalar_tensor_tensor", b("sc_m")[:, :],
                      b("sc_g")[:, :], b("dtT")[:, col:col + 1], b("sc_l")[:, :],
                      wait=ss, op0=OP.mult, op1=OP.mult, inc=sv)
                mk.op("vector", "tensor_scalar", b("sc_cb2")[:, :],
                      b("ones_f")[:, 0:64], b("w2t")[:, col:col + 1], None,
                      op0=OP.mult, inc=sv)
                mk.op("tensor", "transpose", ps_trf[0:64, 128:256],
                      b("sc_cb2")[:, :], b("ident_f_s")[:, :], wait=sv, inc=sp)
                mk.op("vector", "tensor_tensor", b("sc_cw")[:, :],
                      b("c0")[:, s0:s0 + 128], ps_trf[0:64, 128:256],
                      wait=sp, op=OP.mult, inc=sv)
                xtm_h = b("xtm")[:, cl * 2048 + h * 64: cl * 2048 + (h + 1) * 64]
                mk.op("tensor", "matmul", ps_y[0:64, 0:128], xtm_h, b("sc_m")[:, :],
                      wait=sv, wait2=ss, start=True, stop=False)
                mk.op("tensor", "matmul", ps_y[0:64, 0:128],
                      b("hrun_bf")[0:64, h * 64:(h + 1) * 64], b("sc_cw")[:, :],
                      start=False, stop=True, inc=sp)
                mk.op("vector", "tensor_scalar", b("sc_bsc")[:, :],
                      b("btm")[:, cl * 64:(cl + 1) * 64], b("stm")[:, col:col + 1],
                      None, op0=OP.mult, inc=sv)
                mk.op("tensor", "matmul", ps_h[0:64, 0:64], b("sc_bsc")[:, :],
                      xtm_h, wait=sv, start=True, stop=True, inc=sp)
                mk.op("scalar", "copy",
                      b("yasm")[(h % 2) * 64:(h % 2) * 64 + 64,
                                (h // 2) * 128:(h // 2 + 1) * 128],
                      ps_y[0:64, 0:128], wait=sp,
                      wait2=(sv if h == 0 else None), inc=ss)
                mk.op("vector", "scalar_tensor_tensor",
                      b("hrun")[0:64, h * 64:(h + 1) * 64],
                      b("hrun")[0:64, h * 64:(h + 1) * 64],
                      b("dbc")[0:64, col:col + 1], ps_h[0:64, 0:64],
                      wait=sp, op0=OP.mult, op1=OP.add)
                mk.op("vector", "tensor_copy",
                      b("hrun_bf")[0:64, h * 64:(h + 1) * 64],
                      b("hrun")[0:64, h * 64:(h + 1) * 64], inc=sv)
            if self.dbg and q == 0 and cl == 0:
                self.dump("yasm0", b("yasm")[:, :], (128, 2048), F32)
                self.dump("hrun0", b("hrun")[:, :], (64, 2048), F32)
                self.dump("dtT", b("dtT")[:, :], (128, 128), F32)
                self.dump("DtT", b("DtT")[:, :], (128, 128), F32)
                self.dump("stm", b("stm")[:, :], (128, 128), F32)
                self.dump("dbc", b("dbc")[:, :], (128, 128), F32)
                self.dump("w2r", b("w2r")[:, :], (64, 512), F32)
                self.dump("abc", b("abc_s")[:, :], (128, 32), F32)
                self.dump("acol", b("acol_s")[:, :], (32, 1), F32)
                self.dump("sc_g0", b("sc_g")[:, :], (128, 128), F32)
                self.dump("sc_m0", b("sc_m")[:, :], (128, 128), BF16)
                self.dump("btm0", b("btm")[:, :], (128, 256), BF16)
            for j in range(16):
                jy = b("yasm")[:, j * 128:(j + 1) * 128]
                mk.op("vector", "scalar_tensor_tensor", jy,
                      b("xf")[:, j * 512 + s0: j * 512 + s0 + 128],
                      b("d128_s")[:, j:j + 1], jy, wait=ss, op0=OP.mult,
                      op1=OP.add, inc=sv)
                mk.op("scalar", "activation", b("sc_sz")[:, :],
                      b("zf")[:, j * 512 + s0: j * 512 + s0 + 128], AF.Silu,
                      wait=sv, inc=ss)
                mk.op("vector", "tensor_tensor", jy, jy, b("sc_sz")[:, :],
                      wait=ss, op=OP.mult, inc=sv)
                mk.op("scalar", "activation", b("sc_big")[:, 0:128], jy,
                      AF.Square, wait=sv, wait2=sg, inc=ss)
                mk.op("gpsimd", "tensor_reduce", b("ssrow")[0:1, 0:128],
                      b("sc_big")[:, 0:128], wait=ss, axis=AX.C, op=OP.add)
                if j == 0:
                    mk.op("gpsimd", "tensor_copy", b("ssacc")[0:1, :],
                          b("ssrow")[0:1, 0:128], wait=sp, inc=sg)
                else:
                    mk.op("gpsimd", "tensor_tensor", b("ssacc")[0:1, :],
                          b("ssacc")[0:1, :], b("ssrow")[0:1, 0:128],
                          op=OP.add, inc=sg)
            mk.op("scalar", "activation", b("ssacc")[0:1, :], b("ssacc")[0:1, :],
                  AF.Ln, wait=sg, scale=1.0 / 2048, bias=b("c_eps")[0:1, :])
            mk.op("scalar", "activation", b("ssacc")[0:1, :], b("ssacc")[0:1, :],
                  AF.Exp, scale=-0.5, inc=ss)
            mk.op("tensor", "matmul", ps_row[:, 384:512], b("ones_f")[0:1, :],
                  b("ssacc")[0:1, :], wait=ss, wait2=sv, start=True, stop=True,
                  inc=sp)
            for j in range(16):
                jy = b("yasm")[:, j * 128:(j + 1) * 128]
                mk.op("vector", "scalar_tensor_tensor",
                      b("ygf")[:, j * 512 + s0: j * 512 + s0 + 128],
                      jy, b("normw_s")[:, j:j + 1], ps_row[:, 384:512],
                      wait=sp, op0=OP.mult, op1=OP.mult,
                      inc=(sv if j == 15 else None))
            mk.waitop("tensor", sv)

        if self.dbg and q == 0:
            self.dump("ygf", b("ygf")[:, :], (128, 8192), BF16)
            self.dump("dcum", b("dcum")[:, :], (32, 512), F32)
            self.dump("xtm", b("xtm")[:, :], (128, 8192), BF16)
        woutp = b("l0_w_outproj")
        for g0 in range(0, 8, 4):
            mk.waitop("gpsimd", sp)
            for mc in range(g0, g0 + 4):
                mk.dma("gpsimd", b("wbuf")[:, (mc - g0) * 2048:(mc - g0 + 1) * 2048],
                       woutp.ap()[mc, :, :], lw)
            mk.waitop("tensor", lw)
            for mc in range(g0, g0 + 4):
                ps = ps_mm if (mc % 2 == 0) else ps_mm2
                mk.waitop("tensor", ss)
                for k in range(16):
                    mk.op("tensor", "matmul", ps[:, :],
                          b("wbuf")[:, (mc - g0) * 2048 + k * 128:
                                    (mc - g0) * 2048 + (k + 1) * 128],
                          b("ygf")[:, k * 512:(k + 1) * 512],
                          start=(k == 0), stop=(k == 15),
                          inc=(sp if k == 15 else None))
                mk.op("scalar", "copy", b("hf")[:, mc * 512:(mc + 1) * 512],
                      ps[:, :], wait=sp, inc=ss)

        mk.waitop("gpsimd", sp)
        mk.dma("gpsimd", b("wbuf2")[:, :].rearrange("p (k n) -> p k n", k=8),
               b("l0_w_toh").ap().rearrange("(k p) n -> p k n", p=128), lw)
        mk.waitop("tensor", ss)
        for tc in range(4):
            mk.dma("sync", b("sc_big")[:, :],
                   x_hbm[r0 + tc * 128: r0 + (tc + 1) * 128, :], ld,
                   wait=sv, wait2=ss)
            mk.waitop("tensor", lw)
            for n in range(2):
                ps = ps_mm if (n == 0) else ps_mm2
                mk.waitop("tensor", sv)
                for k in range(8):
                    mk.op("tensor", "matmul", ps[:, :],
                          b("hf")[:, k * 512 + tc * 128: k * 512 + (tc + 1) * 128],
                          b("wbuf2")[:, k * 1024 + n * 512: k * 1024 + (n + 1) * 512],
                          start=(k == 0), stop=(k == 7),
                          inc=(sp if k == 7 else None))
                mk.waitop("vector", ld)
                mk.op("vector", "tensor_tensor",
                      b("x1q")[:, tc * 1024 + n * 512: tc * 1024 + (n + 1) * 512],
                      ps[:, :], b("sc_big")[:, n * 512:(n + 1) * 512],
                      wait=sp, op=OP.add, inc=sv)
            mk.dma("sync", x1_hbm[r0 + tc * 128: r0 + (tc + 1) * 128, :],
                   b("x1q")[:, tc * 1024:(tc + 1) * 1024], so, wait=sv)

        if self.dbg and q == 0:
            self.dump("hf", b("hf")[:, :], (128, 4096), BF16)
        for tc in range(4):
            sl = b("x1q")[:, tc * 1024:(tc + 1) * 1024]
            for phase in range(2):
                mk.ln_stats(sl, b("stat_nm")[:, tc:tc + 1],
                            b("stat_rs")[:, tc:tc + 1], 1024, sv, ss,
                            wait=(so if phase == 0 else None))
                if phase == 0:
                    mk.op("vector", "tensor_scalar", sl, sl,
                          b("stat_nm")[:, tc:tc + 1], b("stat_rs")[:, tc:tc + 1],
                          wait=ss, op0=OP.add, op1=OP.mult, inc=sv)
                else:
                    mk.op("vector", "tensor_scalar",
                          b("tmq")[:, tc * 1024:(tc + 1) * 1024], sl,
                          b("stat_nm")[:, tc:tc + 1], b("stat_rs")[:, tc:tc + 1],
                          wait=ss, op0=OP.add, op1=OP.mult, inc=sv)
        for tc in range(4):
            for k in range(8):
                mk.op("tensor", "transpose", ps_tr[:, 0:128],
                      b("tmq")[:, tc * 1024 + k * 128: tc * 1024 + (k + 1) * 128],
                      b("ident_bf_s")[:, :], wait=sv, wait2=ss, inc=sp)
                mk.op("scalar", "copy",
                      b("fmq")[:, k * 512 + tc * 128: k * 512 + (tc + 1) * 128],
                      ps_tr[:, 0:128], wait=sp, inc=ss)
        for k in range(8):
            mk.op("vector", "tensor_scalar", b("fmq")[:, k * 512:(k + 1) * 512],
                  b("fmq")[:, k * 512:(k + 1) * 512], b("ln1w_s")[:, k:k + 1],
                  b("ln1b_s")[:, k:k + 1], wait=ss, op0=OP.mult, op1=OP.add,
                  inc=sv)

        wq = b("l0f_w_qkv")
        mk.waitop("scalar", so)
        for n6 in range(6):
            half = (n6 % 2) * 4096
            mk.waitop("gpsimd", sp)
            mk.dma("gpsimd",
                   b("wbuf2")[:, half:half + 4096].rearrange("p (k n) -> p k n", k=8),
                   wq.ap()[:, n6 * 512:(n6 + 1) * 512]
                   .rearrange("(k p) n -> p k n", p=128), lw)
            mk.waitop("tensor", lw)
            for tc in range(4):
                ps = ps_mm if (tc % 2 == 0) else ps_mm2
                mk.waitop("tensor", ss)
                for k in range(8):
                    mk.op("tensor", "matmul", ps[:, :],
                          b("fmq")[:, k * 512 + tc * 128: k * 512 + (tc + 1) * 128],
                          b("wbuf2")[:, half + k * 512: half + (k + 1) * 512],
                          wait=(sv if k == 0 else None),
                          start=(k == 0), stop=False)
                mk.op("tensor", "matmul", ps[:, :], b("ones_bf")[0:1, :],
                      b("qkvb_s")[0:1, n6 * 512:(n6 + 1) * 512],
                      start=False, stop=True, inc=sp)
                mk.op("scalar", "copy", b("kvstage")[:, :], ps[:, :],
                      wait=sp, wait2=so, inc=ss)
                rr = r0 + tc * 128
                if n6 < 2:
                    dst = q_hbm[rr:rr + 128, n6 * 512:(n6 + 1) * 512]
                else:
                    dst = kv_hbm[rr:rr + 128, (n6 - 2) * 512:(n6 - 1) * 512]
                mk.dma("sync", dst, b("kvstage")[:, :], so, wait=ss)

    def emit(self, x_hbm, x1_hbm, q_hbm, kv_hbm):
        self.alloc()
        self.mk.sbuf("hrun_bf", (64, 32 * 64), BF16)
        self.mk.sbuf("kvstage", (128, 512), BF16)
        self.load_consts()
        for q in range(4):
            self.quarter(q, x_hbm, x1_hbm, q_hbm, kv_hbm)




def emit_top2_gates(mk, glog, gout, scr, renorm, gaps=False):
    b = OP
    sv = mk.sem("sv")
    ss = mk.sem("ss")
    g3 = glog[:, :].rearrange("p (t e) -> p t e", e=8)
    gsh = scr[:, 0:128]
    gsh3 = gsh.rearrange("p (t e) -> p t e", e=8)
    bA = scr[:, 128:256]
    bA3 = bA.rearrange("p (t e) -> p t e", e=8)
    knock = scr[:, 256:384]
    knock3 = knock.rearrange("p (t e) -> p t e", e=8)
    mask = scr[:, 384:512]
    mask3 = mask.rearrange("p (t e) -> p t e", e=8)
    def s16(i):
        return scr[:, 512 + i * 16: 512 + (i + 1) * 16]
    def bc(t16):
        return t16.unsqueeze(2).broadcast_to([128, 16, 8])
    m1, m1b, s_, srec, mm, mmb, m2, m2b, s2, s2b, den, sb = [s16(i) for i in range(12)]

    def vop(*a, **kw):
        if gaps:
            mk.gwait("vector", kw.pop("wait"))
        mk.op("vector", *a, **kw)

    def sop(*a, **kw):
        if gaps and kw.get("wait") is not None:
            mk.gwait("scalar", kw.pop("wait"))
        mk.op("scalar", *a, **kw)

    vop("tensor_reduce", m1, g3, wait=ss, axis=AX.X, op=OP.max, inc=sv)
    sop("copy", m1b, m1, wait=sv, inc=ss)
    vop("tensor_tensor", gsh3, g3, bc(m1b), wait=ss, op=OP.subtract, inc=sv)
    sop("activation", gsh, gsh, AF.Exp, wait=sv, inc=ss)
    vop("tensor_reduce", s_, gsh3, wait=ss, axis=AX.X, op=OP.add, inc=sv)
    sop("copy", sb, s_, wait=sv)
    sop("activation", srec, s_, AF.Ln)
    sop("activation", srec, srec, AF.Exp, scale=-1.0, inc=ss)
    vop("tensor_reduce", mm, gsh3, wait=ss, axis=AX.X, op=OP.max, inc=sv)
    sop("copy", mmb, mm, wait=sv, inc=ss)
    vop("tensor_tensor", mask3, gsh3, bc(mmb), wait=ss, op=OP.is_ge, inc=sv)
    sop("copy", bA, mask, wait=sv, inc=ss)
    vop("scalar_tensor_tensor", knock, bA, -MASK_BIG, gsh,
        wait=ss, op0=OP.mult, op1=OP.add, inc=sv)
    sop("copy", bA, knock, wait=sv, inc=ss)
    vop("tensor_reduce", m2, bA3, wait=ss, axis=AX.X, op=OP.max, inc=sv)
    sop("copy", m2b, m2, wait=sv, inc=ss)
    vop("tensor_tensor", mask3, gsh3, bc(m2b), wait=ss, op=OP.is_ge, inc=sv)
    sop("copy", bA, mask, wait=sv, inc=ss)
    vop("tensor_tensor", knock, gsh, bA, wait=ss, op=OP.mult,
        inc=sv)
    sop("copy", bA, knock, wait=sv, inc=ss)
    if not renorm:
        vop("tensor_tensor", gout[:, :].rearrange("p (t e) -> p t e", e=8),
            bA3, bc(srec), wait=ss, op=OP.mult, inc=sv)
    else:
        vop("tensor_reduce", s2, bA3, wait=ss, axis=AX.X, op=OP.add, inc=sv)
        sop("copy", s2b, s2, wait=sv, inc=ss)
        vop("scalar_tensor_tensor", den, sb, 1e-8, s2b, wait=ss,
            op0=OP.mult, op1=OP.add, inc=sv)
        sop("activation", den, den, AF.Ln, wait=sv)
        sop("activation", den, den, AF.Exp, scale=-1.0, inc=ss)
        vop("tensor_tensor", gout[:, :].rearrange("p (t e) -> p t e", e=8),
            bA3, bc(den), wait=ss, op=OP.mult, inc=sv)



class L0Ffn:

    @staticmethod
    def prep_inputs(inp):
        out = {}
        out["l0f_w_attnout"] = rhs_weight(inp["L0_ffn_out_proj_w"].T)
        out["l0f_attnob"] = inp["L0_ffn_out_proj_b"].reshape(1, 1024).astype(np.float32)
        out["l0f_ln2w"] = fm_cols(inp["L0_ffn_ln2_w"])
        out["l0f_ln2b"] = fm_cols(inp["L0_ffn_ln2_b"])
        out["l0f_w_router"] = rhs_weight(inp["L0_ffn_router_w"].T)
        w1 = inp["L0_ffn_w1"]
        out["l0f_w1"] = np.stack([tile_lhsT(w1[e]) for e in range(8)])
        w2 = inp["L0_ffn_w2"]
        out["l0f_w2"] = np.stack([rhs_weight(w2[e]) for e in range(8)])
        return out

    def __init__(self, mk: MK, dbg=False, moe=True):
        self.mk = mk
        self.dbg = dbg
        self.moe = moe

    def dump(self, name, ap, shape, dtype):
        mk = self.mk
        h = mk.dram_out("dbg2_" + name, shape, dtype)
        for s in ("sv", "ss", "sp", "sg"):
            mk.waitop("sync", mk.sem(s))
        mk.dma("sync", h[tuple(slice(0, n) for n in shape)], ap, mk.sem("so"))

    def alloc(self):
        mk = self.mk
        sb = mk.sbuf
        sb("ident_bf_s", (128, 128), BF16)
        sb("ones_f", (128, 128), F32)
        sb("ones_bf", (128, 128), BF16)
        sb("c_eps", (128, 1), F32)
        sb("x1i", (128, 16 * 1024), F32)
        sb("hn2", (128, 8 * 2048), BF16)
        sb("gates", (128, 16 * 8), F32)
        sb("glog", (128, 16 * 8), F32)
        sb("gscr", (128, 704), F32)
        sb("rmask_bc", (128, 8), F32)
        sb("aob_s", (1, 1024), BF16)
        sb("ln2w_s", (128, 8), F32)
        sb("ln2b_s", (128, 8), F32)
        sb("qtile", (128, 1024), BF16)
        sb("kvt0", (128, 2048), BF16)
        sb("kvt1", (128, 2048), BF16)
        sb("prod", (128, 1024), F32)
        sb("score_s", (128, 128), F32)
        sb("score2", (128, 128), F32)
        sb("attn_acc", (128, 1024), F32)
        sb("attn_bf", (128, 1024), BF16)
        sb("xftile", (128, 1024), F32)
        sb("tmp_tm", (128, 1024), BF16)
        sb("wbuf", (128, 16 * 1024), BF16)
        sb("wbuf2", (128, 16 * 1024), BF16)
        sb("hmid", (128, 16 * 256), BF16)
        sb("stat_nm", (128, 16), F32)
        sb("stat_rs", (128, 16), F32)
        sb("sc_small", (128, 32), F32)
        mk.psum("ps_mm", (128, 512), F32)
        mk.psum("ps_mm2", (128, 512), F32)
        mk.psum("ps_tr", (128, 1024), BF16)
        mk.psum("ps_r", (128, 512), F32)

    def emit(self, x1_hbm, q_hbm, kv_hbm, x2_hbm, poison=False):
        mk = self.mk
        b = mk.buf
        self.alloc()
        if poison:
            mk.poison_all()
        ld = mk.sem("ld")
        lw = mk.sem("lw")
        so = mk.sem("so")
        sv = mk.sem("sv")
        ss = mk.sem("ss")
        sp = mk.sem("sp")
        sg = mk.sem("sg")
        ps_mm, ps_mm2, ps_tr, ps_r = b("ps_mm"), b("ps_mm2"), b("ps_tr"), b("ps_r")

        for nm, dst in [("ident_bf", "ident_bf_s"), ("l0f_ln2w", "ln2w_s"),
                        ("l0f_ln2b", "ln2b_s")]:
            mk.dma("sync", b(dst)[:, :], b(nm).ap(), ld)
        mk.dma("gpsimd", b("aob_s")[:, :], b("l0f_attnob").ap(), lw)
        mk.op("gpsimd", "memset", b("ones_f")[:, :], 1.0, inc=sg)
        mk.op("gpsimd", "memset", b("ones_bf")[:, :], 1.0, inc=sg)
        mk.op("gpsimd", "memset", b("c_eps")[:, :], LN_EPS, inc=sg)
        mk.sbuf("rmrow", (1, 8), F32)
        mk.dma("sync", b("rmrow")[:, :], b("rmask").ap(), ld)
        mk.waitop("tensor", ld)
        mk.waitop("tensor", sg)
        mk.op("tensor", "matmul", ps_r[:, 0:8], b("ones_f")[0:1, :],
              b("rmrow")[0:1, :], start=True, stop=True, inc=sp)
        mk.op("vector", "tensor_copy", b("rmask_bc")[:, :], ps_r[:, 0:8],
              wait=sp, inc=sv)
        for e in ("vector", "scalar"):
            mk.waitop(e, ld)
            mk.waitop(e, lw)
            mk.waitop(e, sg)

        mk.dma("gpsimd", b("wbuf")[:, 0:8192].rearrange("p (k n) -> p k n", k=8),
               b("l0f_w_attnout").ap().rearrange("(k p) n -> p k n", p=128), lw)
        mk.dma("gpsimd", b("wbuf")[:, 8192:8256].rearrange("p (k n) -> p k n", k=8),
               b("l0f_w_router").ap().rearrange("(k p) n -> p k n", p=128), lw)

        for tc in range(16):
            r0 = tc * 128
            mk.dma("sync", b("xftile")[:, :], x1_hbm[r0:r0 + 128, :], ld,
                   wait=sv, wait2=ss)
            mk.dma("sync", b("qtile")[:, :], q_hbm[r0:r0 + 128, :], ld, wait=sv)
            mk.waitop("vector", ld)
            mk.ln_stats(b("xftile"), b("stat_nm")[:, 0:1], b("stat_rs")[:, 0:1],
                        1024, sv, ss, gaps=True)
            mk.gwait("vector", ss)
            mk.op("vector", "tensor_scalar", b("xftile")[:, :], b("xftile")[:, :],
                  b("stat_nm")[:, 0:1], b("stat_rs")[:, 0:1],
                  op0=OP.add, op1=OP.mult, inc=sv)
            stg = b("score_s")
            for r in range(8):
                kvt = b("kvt0") if r % 2 == 0 else b("kvt1")
                mk.dma("sync", kvt[:, :], kv_hbm[r, r0:r0 + 128, :], ld,
                       wait=(sv if r >= 2 else None))
                mk.waitop("vector", ld)
                mk.op("vector", "tensor_tensor", b("prod")[:, :], b("qtile")[:, :],
                      kvt[:, 0:1024], op=OP.mult)
                mk.op("vector", "tensor_reduce", stg[:, r * 16:(r + 1) * 16],
                      b("prod")[:, :].rearrange("p (n d) -> p n d", d=64),
                      axis=AX.X, op=OP.add, inc=sv)
            sc_all = b("score2")
            sc3 = sc_all[:, :].rearrange("p (n r) -> p n r", r=8)
            mk.gwait("scalar", sv)
            mk.op("scalar", "copy", b("prod")[:, 0:128], stg[:, :], inc=ss)
            rm3 = b("rmask_bc")[:, :].unsqueeze(1).broadcast_to([128, 16, 8])
            mk.gwait("vector", ss)
            mk.op("vector", "scalar_tensor_tensor", sc3,
                  b("prod")[:, 0:128].rearrange("p (r n) -> p n r", r=8),
                  0.125, rm3, op0=OP.mult, op1=OP.add, inc=sv)
            mk.gwait("scalar", sv)
            mk.op("scalar", "copy", b("prod")[:, 128:256], sc_all[:, :],
                  inc=ss)
            mx = b("sc_small")[:, 0:16]
            mk.gwait("vector", ss)
            mk.op("vector", "tensor_reduce", mx,
                  b("prod")[:, 128:256].rearrange("p (n r) -> p n r", r=8),
                  axis=AX.X, op=OP.max, inc=sv)
            mk.gwait("scalar", sv)
            mk.op("scalar", "copy", b("sc_small")[:, 16:32].bitcast(F32),
                  mx, inc=ss)
            mk.gwait("vector", ss)
            mk.op("vector", "tensor_tensor", sc3,
                  b("prod")[:, 128:256].rearrange("p (n r) -> p n r", r=8),
                  b("sc_small")[:, 16:32].unsqueeze(2).broadcast_to([128, 16, 8]),
                  op=OP.subtract, inc=sv)
            mk.gwait("scalar", sv)
            mk.op("scalar", "activation", sc_all[:, :], sc_all[:, :], AF.Exp,
                  inc=ss)
            mk.gwait("vector", ss)
            mk.op("vector", "tensor_reduce", mx, sc3, axis=AX.X,
                  op=OP.add, inc=sv)
            mk.gwait("scalar", sv)
            mk.op("scalar", "activation", b("sc_small")[:, 16:32].bitcast(F32),
                  mx, AF.Ln)
            mk.op("scalar", "activation", b("sc_small")[:, 16:32].bitcast(F32),
                  b("sc_small")[:, 16:32].bitcast(F32), AF.Exp, scale=-1.0,
                  inc=ss)
            for r in range(8):
                kvt = b("kvt0") if r % 2 == 0 else b("kvt1")
                mk.dma("sync", kvt[:, :], kv_hbm[r, r0:r0 + 128, :], ld, wait=sv)
                mk.waitop("vector", ld)
                pb = sc3[:, :, r].unsqueeze(2).broadcast_to([128, 16, 64])
                v3 = kvt[:, 1024:2048].rearrange("p (n d) -> p n d", d=64)
                if r == 0:
                    mk.gwait("vector", ss)
                    mk.op("vector", "tensor_tensor",
                          b("attn_acc")[:, :].rearrange("p (n d) -> p n d", d=64),
                          v3, pb, op=OP.mult, inc=sv)
                else:
                    mk.op("vector", "tensor_tensor",
                          b("prod")[:, :].rearrange("p (n d) -> p n d", d=64),
                          v3, pb, op=OP.mult)
                    mk.op("vector", "tensor_tensor", b("attn_acc")[:, :],
                          b("attn_acc")[:, :], b("prod")[:, :],
                          op=OP.add, inc=sv)
            mk.op("vector", "tensor_tensor",
                  b("attn_bf")[:, :].rearrange("p (n d) -> p n d", d=64),
                  b("attn_acc")[:, :].rearrange("p (n d) -> p n d", d=64),
                  b("sc_small")[:, 16:32].unsqueeze(2).broadcast_to([128, 16, 64]),
                  op=OP.mult, inc=sv)
            for k in range(8):
                mk.op("tensor", "transpose", ps_tr[:, 0:128],
                      b("attn_bf")[:, k * 128:(k + 1) * 128],
                      b("ident_bf_s")[:, :], wait=sv, wait2=ss, inc=sp)
                mk.op("scalar", "copy",
                      b("tmp_tm")[:, k * 128:(k + 1) * 128], ps_tr[:, 0:128],
                      wait=sp, inc=ss)
            for n in range(2):
                ps = ps_mm if n == 0 else ps_mm2
                mk.waitop("tensor", sv)
                mk.waitop("tensor", ss)
                for k in range(8):
                    mk.op("tensor", "matmul", ps[:, :],
                          b("tmp_tm")[:, k * 128:(k + 1) * 128],
                          b("wbuf")[:, k * 1024 + n * 512: k * 1024 + (n + 1) * 512],
                          wait=(lw if tc == 0 and n == 0 and k == 0 else None),
                          start=(k == 0), stop=False)
                mk.op("tensor", "matmul", ps[:, :], b("ones_bf")[0:1, :],
                      b("aob_s")[0:1, n * 512:(n + 1) * 512],
                      start=False, stop=True, inc=sp)
                mk.op("vector", "tensor_tensor",
                      b("x1i")[:, tc * 1024 + n * 512: tc * 1024 + (n + 1) * 512],
                      ps[:, :], b("xftile")[:, n * 512:(n + 1) * 512],
                      wait=sp, op=OP.add, inc=sv)
            x1i_t = b("x1i")[:, tc * 1024:(tc + 1) * 1024]
            mk.ln_stats(x1i_t, b("stat_nm")[:, 1:2], b("stat_rs")[:, 1:2],
                        1024, sv, ss, gaps=True)
            mk.gwait("vector", ss)
            mk.op("vector", "tensor_scalar", b("tmp_tm")[:, :], x1i_t,
                  b("stat_nm")[:, 1:2], b("stat_rs")[:, 1:2],
                  op0=OP.add, op1=OP.mult, inc=sv)
            for k in range(8):
                mk.op("tensor", "transpose", ps_tr[:, 0:128],
                      b("tmp_tm")[:, k * 128:(k + 1) * 128],
                      b("ident_bf_s")[:, :], wait=sv, wait2=ss, inc=sp)
                mk.op("scalar", "copy",
                      b("hn2")[:, k * 2048 + tc * 128: k * 2048 + (tc + 1) * 128],
                      ps_tr[:, 0:128], wait=sp, inc=ss)
            mk.gwait("vector", ss)
            for k in range(8):
                sl = b("hn2")[:, k * 2048 + tc * 128: k * 2048 + (tc + 1) * 128]
                mk.op("vector", "tensor_scalar", sl, sl, b("ln2w_s")[:, k:k + 1],
                      b("ln2b_s")[:, k:k + 1], op0=OP.mult, op1=OP.add,
                      inc=sv)
            mk.waitop("tensor", sv)
            for k in range(8):
                mk.op("tensor", "matmul", ps_r[:, 0:8],
                      b("hn2")[:, k * 2048 + tc * 128: k * 2048 + (tc + 1) * 128],
                      b("wbuf")[:, 8192 + k * 8: 8192 + (k + 1) * 8],
                      start=(k == 0), stop=(k == 7), inc=(sp if k == 7 else None))
            mk.op("scalar", "copy", b("glog")[:, tc * 8:(tc + 1) * 8],
                  ps_r[:, 0:8], wait=sp, inc=ss)

        emit_top2_gates(mk, b("glog"), b("gates"), b("gscr"), renorm=False,
                        gaps=True)

        if self.dbg:
            self.dump("x1i_attn", b("x1i")[:, :], (128, 16384), F32)
            self.dump("gates", b("gates")[:, :], (128, 128), F32)
            self.dump("attn_acc", b("attn_acc")[:, :], (128, 1024), F32)
            self.dump("hn2", b("hn2")[:, :], (128, 16384), BF16)
        w1h = b("l0f_w1")
        w2h = b("l0f_w2")
        for e in range(8 if self.moe else 0):
            mk.waitop("gpsimd", sp)
            mk.dma("gpsimd", b("wbuf")[:, :].rearrange("p (m c) -> p m c", m=16),
                   w1h.ap()[e, :, :, :].rearrange("m p c -> p m c"), lw)
            mk.dma("gpsimd", b("wbuf2")[:, :].rearrange("p (k n) -> p k n", k=16),
                   w2h.ap()[e, :, :].rearrange("(k p) n -> p k n", p=128), lw)
            mk.waitop("tensor", lw)
            mk.waitop("tensor", sv)
            for tg in range(8):
                t0 = tg * 256
                for fm in range(16):
                    ps = ps_mm if fm % 2 == 0 else ps_mm2
                    mk.waitop("tensor", ss)
                    for k in range(8):
                        mk.op("tensor", "matmul", ps[:, 0:256],
                              b("wbuf")[:, fm * 1024 + k * 128: fm * 1024 + (k + 1) * 128],
                              b("hn2")[:, k * 2048 + t0: k * 2048 + t0 + 256],
                              start=(k == 0), stop=(k == 7),
                              inc=(sp if k == 7 else None))
                    mk.op("scalar", "activation",
                          b("hmid")[:, fm * 256:(fm + 1) * 256], ps[:, 0:256],
                          AF.Gelu, wait=sp, inc=ss)
                for t2 in range(2):
                    tt = tg * 2 + t2
                    for n in range(2):
                        ps = ps_mm if n == 0 else ps_mm2
                        mk.waitop("tensor", sv)
                        mk.waitop("tensor", ss)
                        for k in range(16):
                            mk.op("tensor", "matmul", ps[:, :],
                                  b("hmid")[:, k * 256 + t2 * 128: k * 256 + t2 * 128 + 128],
                                  b("wbuf2")[:, k * 1024 + n * 512: k * 1024 + (n + 1) * 512],
                                  start=(k == 0), stop=(k == 15),
                                  inc=(sp if k == 15 else None))
                        mk.op("vector", "scalar_tensor_tensor",
                              b("x1i")[:, tt * 1024 + n * 512: tt * 1024 + (n + 1) * 512],
                              ps[:, :], b("gates")[:, tt * 8 + e: tt * 8 + e + 1],
                              b("x1i")[:, tt * 1024 + n * 512: tt * 1024 + (n + 1) * 512],
                              wait=sp, op0=OP.mult, op1=OP.add, inc=sv)

        for tc in range(16):
            mk.dma("sync", b("xftile")[:, :], x1_hbm[tc * 128:(tc + 1) * 128, :],
                   ld, wait=sv)
            mk.waitop("vector", ld)
            mk.op("vector", "tensor_tensor",
                  b("x1i")[:, tc * 1024:(tc + 1) * 1024],
                  b("x1i")[:, tc * 1024:(tc + 1) * 1024], b("xftile")[:, :],
                  op=OP.add, inc=sv)
            mk.dma("sync", x2_hbm[tc * 128:(tc + 1) * 128, :],
                   b("x1i")[:, tc * 1024:(tc + 1) * 1024], so, wait=sv)
        mk.waitop("sync", so)



class L1Attn:
    @staticmethod
    def prep_inputs(inp):
        out = {}
        out["l1_w_inproj"] = tile_lhsT(inp["L1_attn_h_in_proj_w"].T)
        cw = inp["L1_attn_h_conv_w"]
        out["l1_convw"] = np.ascontiguousarray(
            cw.reshape(16, 128, 4).transpose(1, 0, 2).reshape(128, 64).astype(np.float32))
        out["l1_convb"] = fm_cols(inp["L1_attn_h_conv_b"])
        out["l1_w_xproj"] = tile_lhsT(inp["L1_attn_h_x_proj_w"].T)
        out["l1_w_dtproj"] = tile_lhsT(inp["L1_attn_h_dt_proj_w"].T)
        out["l1_dtb"] = fm_cols(inp["L1_attn_h_dt_proj_b"])
        al = inp["L1_attn_h_A_log"]
        out["l1_alog"] = np.ascontiguousarray(
            al.reshape(16, 128, 16).transpose(1, 0, 2).reshape(128, 256).astype(np.float32))
        out["l1_dpar"] = fm_cols(inp["L1_attn_h_D_param"])
        out["l1_w_outproj"] = rhs_weight(inp["L1_attn_h_out_proj_w"].T)
        return out

    def __init__(self, mk: MK, dbg=False):
        self.mk = mk
        self.dbg = dbg

    def dump(self, name, ap, shape, dtype, block=True):
        mk = self.mk
        h = mk.dram_out("dbg3_" + name, shape, dtype)
        for s in ("sv", "ss", "sp", "sg"):
            mk.waitop("sync", mk.sem(s))
        mk.dma("sync", h[tuple(slice(0, n) for n in shape)], ap, mk.sem("so"))
        if block:
            for e in ("vector", "scalar", "tensor", "gpsimd"):
                mk.waitop(e, mk.sem("so"))

    def alloc(self):
        mk = self.mk
        sb = mk.sbuf
        sb("ident_bf_s", (128, 128), BF16)
        sb("ident_f_s", (128, 128), F32)
        sb("ones_f", (128, 128), F32)
        sb("c_eps", (128, 1), F32)
        sb("c_one", (128, 1), F32)
        sb("convw_s", (128, 64), F32)
        sb("convb_s", (128, 16), F32)
        sb("dtb_s", (128, 16), F32)
        sb("a1_s", (128, 256), F32)
        sb("dpar_s", (128, 16), F32)
        sb("xbc1", (128, 16 * 515), BF16)
        sb("z1f", (128, 16 * 512), BF16)
        sb("u1", (128, 16 * 512), BF16)
        sb("du1", (128, 16 * 512), BF16)
        sb("del1", (128, 16 * 512), BF16)
        sb("yac1", (128, 16 * 512), F32)
        sb("ygb1", (128, 16 * 512), BF16)
        sb("xf2q", (128, 8 * 512), BF16)
        sb("tmq", (128, 1024), BF16)
        sb("dtr_rhs", (128, 512), BF16)
        sb("bc1", (32, 512), F32)
        sb("bct", (128, 4 * 32), F32)
        sb("b_bc", (128, 512), F32)
        sb("c_bc", (128, 512), F32)
        sb("carry", (128, 256), F32)
        sb("sc_a", (128, 512), F32)
        sb("sc_b", (128, 512), F32)
        sb("sc_h", (128, 512), F32)
        sb("sc_h2", (128, 512), F32)
        sb("b_bc2", (128, 512), F32)
        sb("c_bc2", (128, 512), F32)
        sb("sc_p", (128, 512), F32)
        sb("sc_r1", (128, 512), F32)
        sb("sc_r2", (128, 512), F32)
        sb("sc_cb", (128, 128), F32)
        sb("xstage", (128, 1024), F32)
        sb("stat_nm", (128, 16), F32)
        sb("stat_rs", (128, 16), F32)
        sb("wbuf", (128, 8 * 1024), BF16)
        sb("wbuf2", (128, 16 * 512), BF16)
        mk.psum("ps_mm", (128, 512), F32)
        mk.psum("ps_mm2", (128, 512), F32)
        mk.psum("ps_tr", (128, 1024), BF16)
        mk.psum("ps_trf", (128, 512), F32)

    def emit(self, x2_hbm, x3_hbm):
        mk = self.mk
        b = mk.buf
        self.alloc()
        ld = mk.sem("ld")
        lw = mk.sem("lw")
        so = mk.sem("so")
        sv = mk.sem("sv")
        ss = mk.sem("ss")
        sp = mk.sem("sp")
        sg = mk.sem("sg")
        ps_mm, ps_mm2 = b("ps_mm"), b("ps_mm2")
        ps_tr, ps_trf = b("ps_tr"), b("ps_trf")

        for nm, dst in [("ident_bf", "ident_bf_s"), ("ident_f", "ident_f_s"),
                        ("l1_convw", "convw_s"), ("l1_convb", "convb_s"),
                        ("l1_dtb", "dtb_s"), ("l1_alog", "a1_s"),
                        ("l1_dpar", "dpar_s")]:
            mk.dma("sync", b(dst)[:, :], b(nm).ap(), ld)
        mk.op("gpsimd", "memset", b("c_eps")[:, :], LN_EPS, inc=sg)
        mk.op("gpsimd", "memset", b("c_one")[:, :], 1.0, inc=sg)
        mk.op("gpsimd", "memset", b("ones_f")[:, :], 1.0, inc=sg)
        mk.op("gpsimd", "memset", b("carry")[:, :], 0.0, inc=sg)
        mk.op("gpsimd", "memset", b("dtr_rhs")[:, :], 0.0, inc=sg)
        for ch in range(16):
            mk.op("gpsimd", "memset", b("xbc1")[:, ch * 515:ch * 515 + 3], 0.0,
                  inc=sg)
        for e in ("vector", "scalar", "tensor"):
            mk.waitop(e, ld)
            mk.waitop(e, sg)
        mk.op("scalar", "activation", b("a1_s")[:, :], b("a1_s")[:, :], AF.Exp,
              inc=ss)
        mk.op("vector", "tensor_scalar", b("a1_s")[:, :], b("a1_s")[:, :],
              -1.0, None, wait=ss, op0=OP.mult, inc=sv)

        for q in range(4):
            r0 = q * 512
            mk.waitop("vector", sp)
            for tc in range(4):
                xs = b("xstage")
                mk.dma("sync", xs[:, :], x2_hbm[r0 + tc * 128: r0 + (tc + 1) * 128, :],
                       ld, wait=sv, wait2=ss)
                mk.waitop("vector", ld)
                mk.ln_stats(xs, b("stat_nm")[:, tc:tc + 1],
                            b("stat_rs")[:, tc:tc + 1], 1024, sv, ss, wait=ld)
                mk.op("vector", "tensor_scalar", b("tmq")[:, 0:1024], xs[:, :],
                      b("stat_nm")[:, tc:tc + 1], b("stat_rs")[:, tc:tc + 1],
                      wait=ss, op0=OP.add, op1=OP.mult, inc=sv)
                for k in range(8):
                    mk.op("tensor", "transpose", ps_tr[:, 0:128],
                          b("tmq")[:, k * 128:(k + 1) * 128],
                          b("ident_bf_s")[:, :], wait=sv, wait2=ss, inc=sp)
                    mk.op("scalar", "copy",
                          b("xf2q")[:, k * 512 + tc * 128: k * 512 + (tc + 1) * 128],
                          ps_tr[:, 0:128], wait=sp, inc=ss)

            winp = b("l1_w_inproj")
            mk.waitop("scalar", sv)
            G = 8
            for g0 in range(0, 32, G):
                mk.waitop("gpsimd", sp)
                for mc in range(g0, g0 + G):
                    mk.dma("gpsimd", b("wbuf")[:, (mc - g0) * 1024:(mc - g0 + 1) * 1024],
                           winp.ap()[mc, :, :], lw)
                mk.waitop("tensor", lw)
                for mc in range(g0, g0 + G):
                    ps = ps_mm if (mc % 2 == 0) else ps_mm2
                    mk.waitop("tensor", ss)
                    for k in range(8):
                        mk.op("tensor", "matmul", ps[:, :],
                              b("wbuf")[:, (mc - g0) * 1024 + k * 128:
                                        (mc - g0) * 1024 + (k + 1) * 128],
                              b("xf2q")[:, k * 512:(k + 1) * 512],
                              start=(k == 0), stop=(k == 7),
                              inc=(sp if k == 7 else None))
                    if mc < 16:
                        mk.op("scalar", "copy",
                              b("xbc1")[:, mc * 515 + 3: mc * 515 + 515],
                              ps[:, :], wait=sp, inc=ss)
                    else:
                        mk.op("scalar", "copy",
                              b("z1f")[:, (mc - 16) * 512:(mc - 15) * 512],
                              ps[:, :], wait=sp, inc=ss)

            for ch in range(16):
                base = ch * 515
                acc = b("sc_p")
                mk.op("vector", "tensor_scalar", acc[:, :],
                      b("xbc1")[:, base:base + 512],
                      b("convw_s")[:, ch * 4: ch * 4 + 1], None, wait=ss,
                      op0=OP.mult)
                for kk in range(1, 4):
                    mk.op("vector", "scalar_tensor_tensor", acc[:, :],
                          b("xbc1")[:, base + kk: base + kk + 512],
                          b("convw_s")[:, ch * 4 + kk: ch * 4 + kk + 1], acc[:, :],
                          op0=OP.mult, op1=OP.add,
                          inc=(sv if kk == 3 else None))
                mk.op("scalar", "activation", b("u1")[:, ch * 512:(ch + 1) * 512],
                      acc[:, :], AF.Silu, wait=sv,
                      bias=b("convb_s")[:, ch:ch + 1], inc=ss)
            for ch in range(16):
                base = ch * 515
                mk.op("vector", "tensor_copy", b("xbc1")[:, base:base + 3],
                      b("xbc1")[:, base + 512:base + 515], wait=ss,
                      inc=(sv if ch == 15 else None))

            wxp = b("l1_w_xproj")
            mk.waitop("gpsimd", sp)
            mk.dma("gpsimd", b("wbuf")[:, 0:2048], wxp.ap()[0, :, :], lw)
            mk.waitop("tensor", lw)
            mk.waitop("tensor", ss)
            for k in range(16):
                mk.op("tensor", "matmul", ps_mm[:, :],
                      b("wbuf")[:, k * 128:(k + 1) * 128],
                      b("u1")[:, k * 512:(k + 1) * 512],
                      start=(k == 0), stop=(k == 15), inc=(sp if k == 15 else None))
            mk.op("scalar", "copy", b("dtr_rhs")[0:64, :], ps_mm[0:64, :],
                  wait=sp, inc=ss)
            mk.op("scalar", "copy", b("bc1")[:, :], ps_mm[64:96, :], inc=ss)

            wdt = b("l1_w_dtproj")
            mk.waitop("gpsimd", sp)
            mk.dma("gpsimd",
                   b("wbuf")[:, 2048:4096].rearrange("p (m c) -> p m c", m=16),
                   wdt.ap().rearrange("m p c -> p m c"), lw)
            mk.waitop("tensor", lw)
            for md in range(16):
                ps = ps_mm if (md % 2 == 0) else ps_mm2
                mk.waitop("tensor", ss)
                mk.op("tensor", "matmul", ps[:, :],
                      b("wbuf")[:, 2048 + md * 128: 2048 + (md + 1) * 128],
                      b("dtr_rhs")[:, :], start=True, stop=True, inc=sp)
                mk.op("scalar", "activation", b("sc_r1")[:, :], ps[:, :], AF.Relu,
                      wait=sp, bias=b("dtb_s")[:, md:md + 1])
                mk.op("scalar", "activation", b("sc_r2")[:, :], ps[:, :], AF.Abs,
                      bias=b("dtb_s")[:, md:md + 1])
                mk.op("scalar", "activation", b("sc_r2")[:, :], b("sc_r2")[:, :],
                      AF.Exp, scale=-1.0)
                mk.op("scalar", "activation", b("sc_r2")[:, :], b("sc_r2")[:, :],
                      AF.Ln, bias=b("c_one")[:, :], inc=ss)
                mk.op("vector", "tensor_tensor",
                      b("del1")[:, md * 512:(md + 1) * 512], b("sc_r1")[:, :],
                      b("sc_r2")[:, :], wait=ss, op=OP.add, inc=sv)
                mk.op("vector", "tensor_tensor",
                      b("du1")[:, md * 512:(md + 1) * 512],
                      b("del1")[:, md * 512:(md + 1) * 512],
                      b("u1")[:, md * 512:(md + 1) * 512], op=OP.mult, inc=sv)

            if self.dbg and q == 0:
                self.dump("u1", b("u1")[:, :], (128, 8192), BF16)
                self.dump("z1f", b("z1f")[:, :], (128, 8192), BF16)
                self.dump("del1", b("del1")[:, :], (128, 8192), BF16)
                self.dump("du1", b("du1")[:, :], (128, 8192), BF16)
                self.dump("dtrr", b("dtr_rhs")[:, :], (128, 512), BF16)
                self.dump("bc1d", b("bc1")[:, :], (32, 512), F32)
                self.dump("xf2q", b("xf2q")[:, :], (128, 4096), BF16)
            for tch in range(4):
                mk.op("tensor", "transpose", ps_trf[:, 0:32],
                      b("bc1")[:, tch * 128:(tch + 1) * 128],
                      b("ident_f_s")[0:32, 0:32], wait=sv, wait2=ss, inc=sp)
                mk.op("scalar", "copy", b("bct")[:, tch * 32:(tch + 1) * 32],
                      ps_trf[:, 0:32], wait=sp, inc=ss)
            def consume(k):
                nn, dd = divmod(k, 16)
                gcol2 = dd * 16 + nn
                hbuf = b("sc_h") if k % 2 == 0 else b("sc_h2")
                cbuf = b("c_bc") if nn % 2 == 0 else b("c_bc2")
                if nn == 0:
                    mk.op("vector", "tensor_tensor",
                          b("yac1")[:, dd * 512:(dd + 1) * 512],
                          hbuf[:, :], cbuf[:, :], op=OP.mult, inc=sv)
                else:
                    mk.op("vector", "tensor_tensor", b("sc_p")[:, :],
                          hbuf[:, :], cbuf[:, :], op=OP.mult)
                    mk.op("vector", "tensor_tensor",
                          b("yac1")[:, dd * 512:(dd + 1) * 512],
                          b("yac1")[:, dd * 512:(dd + 1) * 512],
                          b("sc_p")[:, :], op=OP.add, inc=sv)
                mk.op("scalar", "copy", b("carry")[:, gcol2:gcol2 + 1],
                      hbuf[:, 511:512], wait=sv, inc=ss)

            for n in range(16):
                bbuf = b("b_bc") if n % 2 == 0 else b("b_bc2")
                cbuf = b("c_bc") if n % 2 == 0 else b("c_bc2")
                for tch in range(4):
                    mk.op("vector", "tensor_scalar", b("sc_cb")[:, :],
                          b("ones_f")[:, :], b("bct")[:, tch * 32 + n: tch * 32 + n + 1],
                          None, wait=sp, wait2=ss, op0=OP.mult, inc=sv)
                    mk.op("tensor", "transpose", ps_trf[:, 128:256],
                          b("sc_cb")[:, :], b("ident_f_s")[:, :], wait=sv, inc=sp)
                    mk.op("vector", "tensor_copy",
                          bbuf[:, tch * 128:(tch + 1) * 128],
                          ps_trf[:, 128:256], wait=sp, inc=sv)
                    mk.op("vector", "tensor_scalar", b("sc_cb")[:, :],
                          b("ones_f")[:, :],
                          b("bct")[:, tch * 32 + 16 + n: tch * 32 + 17 + n],
                          None, op0=OP.mult, inc=sv)
                    mk.op("tensor", "transpose", ps_trf[:, 128:256],
                          b("sc_cb")[:, :], b("ident_f_s")[:, :], wait=sv, inc=sp)
                    mk.op("vector", "tensor_copy",
                          cbuf[:, tch * 128:(tch + 1) * 128],
                          ps_trf[:, 128:256], wait=sp, inc=sv)
                for dblk in range(16):
                    k = n * 16 + dblk
                    gcol = dblk * 16 + n
                    hbuf = b("sc_h") if k % 2 == 0 else b("sc_h2")
                    mk.op("scalar", "activation", b("sc_a")[:, :],
                          b("del1")[:, dblk * 512:(dblk + 1) * 512], AF.Exp,
                          wait=sv, scale=b("a1_s")[:, gcol:gcol + 1], inc=ss)
                    mk.op("vector", "tensor_tensor", b("sc_b")[:, :],
                          b("du1")[:, dblk * 512:(dblk + 1) * 512], bbuf[:, :],
                          wait=ss, op=OP.mult, inc=sv)
                    mk.op("vector", "tensor_tensor_scan", hbuf[:, :],
                          b("sc_a")[:, :], b("sc_b")[:, :],
                          b("carry")[:, gcol:gcol + 1], op0=OP.mult, op1=OP.add,
                          inc=sv)
                    if k > 0:
                        consume(k - 1)
                    if self.dbg and q == 0 and k in (32, 128, 255):
                        self.dump("yk%d" % k, b("yac1")[:, :], (128, 8192), F32)
            consume(255)

            if self.dbg and q == 0:
                self.dump("yac1", b("yac1")[:, :], (128, 8192), F32)
                self.dump("carry", b("carry")[:, :], (128, 256), F32)
                self.dump("sca", b("sc_a")[:, :], (128, 512), F32)
                self.dump("scb", b("sc_b")[:, :], (128, 512), F32)
                self.dump("sch2", b("sc_h2")[:, :], (128, 512), F32)
                self.dump("bbc2", b("b_bc2")[:, :], (128, 512), F32)
                self.dump("cbc2", b("c_bc2")[:, :], (128, 512), F32)
            for dblk in range(16):
                mk.op("vector", "scalar_tensor_tensor",
                      b("yac1")[:, dblk * 512:(dblk + 1) * 512],
                      b("u1")[:, dblk * 512:(dblk + 1) * 512],
                      b("dpar_s")[:, dblk:dblk + 1],
                      b("yac1")[:, dblk * 512:(dblk + 1) * 512],
                      op0=OP.mult, op1=OP.add, inc=sv)
                mk.op("scalar", "activation", b("sc_r1")[:, :],
                      b("z1f")[:, dblk * 512:(dblk + 1) * 512], AF.Silu,
                      wait=sv, inc=ss)
                mk.op("vector", "tensor_tensor",
                      b("ygb1")[:, dblk * 512:(dblk + 1) * 512],
                      b("yac1")[:, dblk * 512:(dblk + 1) * 512], b("sc_r1")[:, :],
                      wait=ss, op=OP.mult, inc=sv)

            if self.dbg and q == 0:
                self.dump("ygb1", b("ygb1")[:, :], (128, 8192), BF16)
                self.dump("yach", b("yac1")[:, :], (128, 8192), F32)
            wop = b("l1_w_outproj")
            for n in range(2):
                mk.waitop("gpsimd", sp)
                mk.dma("gpsimd",
                       b("wbuf2")[:, :].rearrange("p (k c) -> p k c", k=16),
                       wop.ap()[:, n * 512:(n + 1) * 512]
                       .rearrange("(k p) c -> p k c", p=128), lw)
                mk.waitop("tensor", lw)
                for tc in range(4):
                    ps = ps_mm if (tc % 2 == 0) else ps_mm2
                    mk.dma("sync", b("xstage")[:, n * 512:(n + 1) * 512],
                           x2_hbm[r0 + tc * 128: r0 + (tc + 1) * 128,
                                  n * 512:(n + 1) * 512], ld, wait=sv)
                    mk.waitop("tensor", ss)
                    mk.waitop("tensor", sv)
                    for k in range(16):
                        mk.op("tensor", "matmul", ps[:, :],
                              b("ygb1")[:, k * 512 + tc * 128: k * 512 + (tc + 1) * 128],
                              b("wbuf2")[:, k * 512:(k + 1) * 512],
                              start=(k == 0), stop=(k == 15),
                              inc=(sp if k == 15 else None))
                    mk.waitop("vector", ld)
                    mk.op("vector", "tensor_tensor", b("sc_p")[:, :], ps[:, :],
                          b("xstage")[:, n * 512:(n + 1) * 512], wait=sp,
                          wait2=so, op=OP.add, inc=sv)
                    mk.dma("sync",
                           x3_hbm[r0 + tc * 128: r0 + (tc + 1) * 128,
                                  n * 512:(n + 1) * 512],
                           b("sc_p")[:, :], so, wait=sv)
            mk.waitop("vector", so)
        mk.waitop("sync", so)



class L1Ffn:
    @staticmethod
    def prep_inputs(inp):
        out = {}
        out["l1f_w_router"] = rhs_weight(inp["L1_ffn_gate_weight"].T)
        sgu = inp["L1_ffn_shared_gate_up"]
        egu = inp["L1_ffn_experts_gate_up"]
        sd = inp["L1_ffn_shared_down"]
        ed = inp["L1_ffn_experts_down"]
        wg = [tile_lhsT(sgu[:938].T)] + [tile_lhsT(egu[e, :938].T) for e in range(8)]
        wu = [tile_lhsT(sgu[938:].T)] + [tile_lhsT(egu[e, 938:].T) for e in range(8)]
        wd = [rhs_weight(sd.T)] + [rhs_weight(ed[e].T) for e in range(8)]
        out["l1f_wg"] = np.stack(wg)
        out["l1f_wu"] = np.stack(wu)
        out["l1f_wd"] = np.stack(wd)
        return out

    def __init__(self, mk: MK, dbg=False):
        self.mk = mk
        self.dbg = dbg

    def alloc(self):
        mk = self.mk
        sb = mk.sbuf
        sb("ident_bf_s", (128, 128), BF16)
        sb("c_eps", (128, 1), F32)
        sb("xf3", (128, 8 * 2048), BF16)
        sb("oacc", (128, 16 * 1024), F32)
        sb("cw", (128, 16 * 8), F32)
        sb("glog", (128, 16 * 8), F32)
        sb("gscr", (128, 704), F32)
        sb("gblk", (128, 8 * 256), BF16)
        sb("ublk", (128, 8 * 256), BF16)
        sb("midb", (128, 8 * 256), BF16)
        sb("xstage", (128, 1024), F32)
        sb("tmp_tm", (128, 1024), BF16)
        sb("stat_nm", (128, 16), F32)
        sb("stat_rs", (128, 16), F32)
        sb("sc_small", (128, 32), F32)
        sb("wbuf", (128, 16384), BF16)
        sb("wbuf2", (128, 8192), BF16)
        mk.psum("ps_mm", (128, 512), F32)
        mk.psum("ps_mm2", (128, 512), F32)
        mk.psum("ps_tr", (128, 1024), BF16)
        mk.psum("ps_r", (128, 512), F32)

    def emit(self, x3_hbm, out_hbm):
        mk = self.mk
        b = mk.buf
        self.alloc()
        ld = mk.sem("ld")
        lw = mk.sem("lw")
        so = mk.sem("so")
        sv = mk.sem("sv")
        ss = mk.sem("ss")
        sp = mk.sem("sp")
        sg = mk.sem("sg")
        ps_mm, ps_mm2, ps_tr, ps_r = b("ps_mm"), b("ps_mm2"), b("ps_tr"), b("ps_r")

        mk.dma("sync", b("ident_bf_s")[:, :], b("ident_bf").ap(), ld)
        mk.op("gpsimd", "memset", b("c_eps")[:, :], LN_EPS, inc=sg)
        mk.dma("gpsimd", b("wbuf2")[:, 0:64].rearrange("p (k n) -> p k n", k=8),
               b("l1f_w_router").ap().rearrange("(k p) n -> p k n", p=128), lw)
        for e in ("vector", "scalar", "tensor"):
            mk.waitop(e, ld)
            mk.waitop(e, lw)
            mk.waitop(e, sg)

        for tc in range(16):
            xs = b("xstage")
            mk.dma("sync", xs[:, :], x3_hbm[tc * 128:(tc + 1) * 128, :], ld,
                   wait=sv, wait2=ss)
            mk.waitop("vector", ld)
            mk.op("vector", "tensor_copy", b("oacc")[:, tc * 1024:(tc + 1) * 1024],
                  xs[:, :], inc=sv)
            mk.ln_stats(xs, b("stat_nm")[:, 0:1], b("stat_rs")[:, 0:1],
                        1024, sv, ss)
            mk.op("vector", "tensor_scalar", b("tmp_tm")[:, :], xs[:, :],
                  b("stat_nm")[:, 0:1], b("stat_rs")[:, 0:1], wait=ss,
                  op0=OP.add, op1=OP.mult, inc=sv)
            for k in range(8):
                mk.op("tensor", "transpose", ps_tr[:, 0:128],
                      b("tmp_tm")[:, k * 128:(k + 1) * 128],
                      b("ident_bf_s")[:, :], wait=sv, wait2=ss, inc=sp)
                mk.op("scalar", "copy",
                      b("xf3")[:, k * 2048 + tc * 128: k * 2048 + (tc + 1) * 128],
                      ps_tr[:, 0:128], wait=sp, inc=ss)
            mk.waitop("tensor", ss)
            for k in range(8):
                mk.op("tensor", "matmul", ps_r[:, 0:8],
                      b("xf3")[:, k * 2048 + tc * 128: k * 2048 + (tc + 1) * 128],
                      b("wbuf2")[:, k * 8:(k + 1) * 8],
                      start=(k == 0), stop=(k == 7), inc=(sp if k == 7 else None))
            mk.op("scalar", "copy", b("glog")[:, tc * 8:(tc + 1) * 8],
                  ps_r[:, 0:8], wait=sp, inc=ss)

        emit_top2_gates(mk, b("glog"), b("cw"), b("gscr"), renorm=True)

        wgh, wuh, wdh = b("l1f_wg"), b("l1f_wu"), b("l1f_wd")
        for j in range(9):
            mk.waitop("gpsimd", sp)
            mk.dma("gpsimd", b("wbuf")[:, 0:8192].rearrange("p (m c) -> p m c", m=8),
                   wgh.ap()[j, :, :, :].rearrange("m p c -> p m c"), lw)
            mk.dma("gpsimd", b("wbuf")[:, 8192:16384].rearrange("p (m c) -> p m c", m=8),
                   wuh.ap()[j, :, :, :].rearrange("m p c -> p m c"), lw)
            mk.dma("gpsimd", b("wbuf2")[:, 0:8192].rearrange("p (k n) -> p k n", k=8),
                   wdh.ap()[j, :, :].rearrange("(k p) n -> p k n", p=128), lw)
            mk.waitop("tensor", lw)
            mk.waitop("tensor", sg)
            for tg in range(8):
                t0 = tg * 256
                for mc in range(8):
                    ps = ps_mm if mc % 2 == 0 else ps_mm2
                    mk.waitop("tensor", ss)
                    mk.waitop("tensor", sv)
                    for k in range(8):
                        mk.op("tensor", "matmul", ps[:, 0:256],
                              b("wbuf")[:, mc * 1024 + k * 128: mc * 1024 + (k + 1) * 128],
                              b("xf3")[:, k * 2048 + t0: k * 2048 + t0 + 256],
                              start=(k == 0), stop=(k == 7),
                              inc=(sp if k == 7 else None))
                    mk.op("scalar", "activation",
                          b("gblk")[:, mc * 256:(mc + 1) * 256], ps[:, 0:256],
                          AF.Silu, wait=sp, inc=ss)
                    ps2 = ps_mm2 if mc % 2 == 0 else ps_mm
                    for k in range(8):
                        mk.op("tensor", "matmul", ps2[:, 0:256],
                              b("wbuf")[:, 8192 + mc * 1024 + k * 128:
                                        8192 + mc * 1024 + (k + 1) * 128],
                              b("xf3")[:, k * 2048 + t0: k * 2048 + t0 + 256],
                              start=(k == 0), stop=(k == 7),
                              inc=(sp if k == 7 else None))
                    mk.op("scalar", "copy",
                          b("ublk")[:, mc * 256:(mc + 1) * 256], ps2[:, 0:256],
                          wait=sp, inc=ss)
                    mk.op("vector", "tensor_tensor",
                          b("midb")[:, mc * 256:(mc + 1) * 256],
                          b("gblk")[:, mc * 256:(mc + 1) * 256],
                          b("ublk")[:, mc * 256:(mc + 1) * 256], wait=ss,
                          op=OP.mult, inc=sv)
                for t2 in range(2):
                    tt = tg * 2 + t2
                    for n in range(2):
                        ps = ps_mm if n == 0 else ps_mm2
                        mk.waitop("tensor", sv)
                        mk.waitop("tensor", ss)
                        for k in range(8):
                            mk.op("tensor", "matmul", ps[:, :],
                                  b("midb")[:, k * 256 + t2 * 128: k * 256 + t2 * 128 + 128],
                                  b("wbuf2")[:, k * 1024 + n * 512: k * 1024 + (n + 1) * 512],
                                  start=(k == 0), stop=(k == 7),
                                  inc=(sp if k == 7 else None))
                        dst = b("oacc")[:, tt * 1024 + n * 512: tt * 1024 + (n + 1) * 512]
                        if j == 0:
                            mk.op("vector", "tensor_tensor", dst, dst, ps[:, :],
                                  wait=sp, op=OP.add, inc=sv)
                        else:
                            mk.op("vector", "scalar_tensor_tensor", dst, ps[:, :],
                                  b("cw")[:, tt * 8 + (j - 1): tt * 8 + j], dst,
                                  wait=sp, op0=OP.mult, op1=OP.add, inc=sv)

        for tc in range(16):
            mk.dma("sync", out_hbm[tc * 128:(tc + 1) * 128, :],
                   b("oacc")[:, tc * 1024:(tc + 1) * 1024], so, wait=sv)
        mk.waitop("sync", so)



_CACHE = {}


def _build_l0attn():
    nc = bass.Bass("TRN2", target_bir_lowering=False)
    mkk = MK(nc)
    x_h = mkk.dram_in("x", np.zeros((2048, 1024), np.float32))
    x1_h = mkk.dram_out("x1", (2048, 1024), F32)
    q_h = mkk.dram_out("qout", (2048, 1024), BF16)
    kv_h = mkk.dram_out("kvout", (2048, 2048), BF16)
    for k in ("l0_w_inproj", "l0_convw", "l0_convb", "l0_dtbias", "l0_alog",
              "l0_d128", "l0_normw", "l0_w_outproj", "l0_w_toh", "l0f_ln1w",
              "l0f_ln1b", "l0f_w_qkv", "l0f_qkvb", "cmask", "ident_bf",
              "ident_f"):
        mkk.dram_in(k, _CACHE["prep"][k])
    with mkk.ctx:
        with nc.Block() as block:
            L0Attn(mkk).emit(x_h, x1_h, q_h, kv_h)
            mkk.finalize_and_emit(block)
    return nc, dict(mkk.inputs)


def _build_l0ffn():
    nc = bass.Bass("TRN2", target_bir_lowering=False)
    mkk = MK(nc)
    x1_h = mkk.dram_in("x1in", np.zeros((2048, 1024), np.float32))
    q_h = mkk.dram_in("qin", np.zeros((2048, 1024), ml_dtypes.bfloat16))
    kv_h = mkk.dram_in("kvall", np.zeros((8, 2048, 2048), ml_dtypes.bfloat16))
    rm_h = mkk.dram_in("rmask", np.zeros((1, 8), np.float32))
    x2_h = mkk.dram_out("x2", (2048, 1024), F32)
    for k in ("l0f_w_attnout", "l0f_attnob", "l0f_ln2w", "l0f_ln2b",
              "l0f_w_router", "l0f_w1", "l0f_w2", "ident_bf"):
        mkk.dram_in(k, _CACHE["prep"][k])
    with mkk.ctx:
        with nc.Block() as block:
            L0Ffn(mkk).emit(x1_h, q_h, kv_h, x2_h)
            mkk.finalize_and_emit(block)
    return nc, dict(mkk.inputs)


def _build_l1attn():
    nc = bass.Bass("TRN2", target_bir_lowering=False)
    mkk = MK(nc)
    x2_h = mkk.dram_in("x2in", np.zeros((2048, 1024), np.float32))
    x3_h = mkk.dram_out("x3", (2048, 1024), F32)
    for k in ("l1_w_inproj", "l1_convw", "l1_convb", "l1_w_xproj",
              "l1_w_dtproj", "l1_dtb", "l1_alog", "l1_dpar", "l1_w_outproj",
              "ident_bf", "ident_f"):
        mkk.dram_in(k, _CACHE["prep"][k])
    with mkk.ctx:
        with nc.Block() as block:
            L1Attn(mkk).emit(x2_h, x3_h)
            mkk.finalize_and_emit(block)
    return nc, dict(mkk.inputs)


def _build_l1ffn():
    nc = bass.Bass("TRN2", target_bir_lowering=False)
    mkk = MK(nc)
    x3_h = mkk.dram_in("x3in", np.zeros((2048, 1024), np.float32))
    o_h = mkk.dram_out("out", (2048, 1024), F32)
    for k in ("l1f_w_router", "l1f_wg", "l1f_wu", "l1f_wd", "ident_bf"):
        mkk.dram_in(k, _CACHE["prep"][k])
    with mkk.ctx:
        with nc.Block() as block:
            L1Ffn(mkk).emit(x3_h, o_h)
            mkk.finalize_and_emit(block)
    return nc, dict(mkk.inputs)


def _prep_all(inputs):
    prep = {}
    prep.update(L0Attn.prep_inputs(inputs))
    prep.update(L0Ffn.prep_inputs(inputs))
    prep.update(L1Attn.prep_inputs(inputs))
    prep.update(L1Ffn.prep_inputs(inputs))
    prep["ident_bf"] = np.eye(128, dtype=ml_dtypes.bfloat16)
    prep["ident_f"] = np.eye(128, dtype=np.float32)
    return prep


def kernel(**inputs):
    _CACHE["prep"] = _prep_all(inputs)
    if "p1" not in _CACHE:
        _CACHE["p1"] = _build_l0attn()
        _CACHE["p2"] = _build_l0ffn()
        _CACHE["p3"] = _build_l1attn()
        _CACHE["p4"] = _build_l1ffn()
    else:
        for key in ("p1", "p2", "p3", "p4"):
            nc, im = _CACHE[key]
            for k in list(im.keys()):
                if k in _CACHE["prep"]:
                    im[k] = _CACHE["prep"][k]

    X = np.asarray(inputs["L0_attn_X"], np.float32)

    def run2(nc, maps):
        run_bass_kernel_spmd(nc, maps, core_ids=list(range(8)))
        return run_bass_kernel_spmd(nc, maps, core_ids=list(range(8))).results

    nc1, im1 = _CACHE["p1"]
    maps = []
    for c in range(8):
        m = dict(im1)
        m["x"] = np.ascontiguousarray(X[c])
        maps.append(m)
    r1 = run_bass_kernel_spmd(nc1, maps, core_ids=list(range(8))).results

    kv_all = np.ascontiguousarray(np.stack([r["kvout"] for r in r1]))

    nc2, im2 = _CACHE["p2"]
    maps = []
    for c in range(8):
        m = dict(im2)
        m["x1in"] = r1[c]["x1"]
        m["qin"] = r1[c]["qout"]
        m["kvall"] = kv_all
        rm = np.full((1, 8), -30000.0, np.float32)
        rm[0, :c + 1] = 0.0
        m["rmask"] = rm
        maps.append(m)
    r2 = run2(nc2, maps)

    nc3, im3 = _CACHE["p3"]
    maps = []
    for c in range(8):
        m = dict(im3)
        m["x2in"] = r2[c]["x2"]
        maps.append(m)
    r3 = run_bass_kernel_spmd(nc3, maps, core_ids=list(range(8))).results

    nc4, im4 = _CACHE["p4"]
    maps = []
    for c in range(8):
        m = dict(im4)
        m["x3in"] = r3[c]["x3"]
        maps.append(m)
    r4 = run_bass_kernel_spmd(nc4, maps, core_ids=list(range(8))).results

    return np.ascontiguousarray(np.stack([r["out"] for r in r4]))
